# version 10
# baseline (speedup 1.0000x reference)
import numpy as np
from collections import defaultdict
from contextlib import ExitStack

import concourse.bass as bass
import concourse.mybir as mybir
from concourse.bass_utils import run_bass_kernel_spmd

F32 = mybir.dt.float32
BF16 = mybir.dt.bfloat16
AF = mybir.ActivationFunctionType
ALU = mybir.AluOpType
AX = mybir.AxisListType.X

E = 1024
FF = 4096
EC = E // 128
FC = FF // 128
NT = 512
T_TOTAL = 8192
N_CORES = 8

LN0_EPS = 1e-5
LN_EPS = 1e-3
ENGS = ("tensor", "vector", "scalar", "sync", "gpsimd")

WSPEC = ("wkvr", "wo", "cwr", "cwk", "w1t", "cwv", "w2t")
POOLS = {"pkvr": (2, 3072),
         "po": (2, 2048),
         "pk": (3, 2048),
         "pv": (2, 4096)}
WPOOL = {"wkvr": "pkvr", "wo": "po", "cwr": "po",
         "cwk": "pk", "w1t": "pk", "cwv": "pv", "w2t": "pv"}
WQUEUE = {"wkvr": "sync", "wo": "sync", "cwr": "sync",
          "cwk": "sync", "w1t": "sync", "cwv": "sync", "w2t": "sync"}


class Builder:
    def __init__(self):
        self.ev = {k: [] for k in ENGS}
        self.ctr = defaultdict(int)
        self.waited = {k: defaultdict(int) for k in ENGS}

    def emit(self, eng, fn, waits=(), inc=None, incn=1):
        ws = []
        for s, v in waits:
            if v > self.waited[eng][s]:
                self.waited[eng][s] = v
                ws.append((s, v))
        if inc is not None:
            self.ctr[inc] += incn
        self.ev[eng].append((ws, fn, inc, incn))
        return self.ctr[inc] if inc is not None else None

    def val(self, sem):
        return self.ctr[sem]


def build_program(t_total=T_TOTAL, dbg=False):
    n_tiles = t_total // NT
    nc = bass.Bass("TRN2", target_bir_lowering=False)

    xS = nc.dram_tensor("xS", [n_tiles, 128, EC * NT], F32, kind="ExternalInput").ap()
    outS = nc.dram_tensor("outS", [n_tiles, 128, EC * NT], F32, kind="ExternalOutput").ap()

    NFILLS = {"wkvr": 8, "wo": 4, "cwr": 4,
              "cwk": 16, "w1t": 16, "cwv": 8, "w2t": 8}
    FCOLS = {"wkvr": 3072, "wo": 2048, "cwr": 2048,
             "cwk": 2048, "w1t": 2048, "cwv": 4096, "w2t": 4096}
    wdram, wscr = {}, {}
    for w in WSPEC:
        wdram[w] = nc.dram_tensor(w, [NFILLS[w] * 128, FCOLS[w]], F32, kind="ExternalInput").ap()
        wscr[w] = nc.dram_tensor("z" + w, [NFILLS[w] * 128, FCOLS[w]], BF16, kind="Internal").ap()

    VEC = {"g1": EC, "b1v": EC, "g2": EC, "b2v": EC, "mk": EC, "mv": EC,
           "mr": EC, "cmk": EC, "cmr": EC, "td": EC, "tf": EC, "b2f": EC,
           "b1f": FC}
    vdram = {v: nc.dram_tensor(v, [128, n], F32, kind="ExternalInput").ap()
             for v, n in VEC.items()}

    ctx = ExitStack()
    e = ctx.enter_context
    sb = {}

    def S(name, shape, dt=F32):
        sb[name] = e(nc.sbuf_tensor(name, shape, dt))
        return sb[name]

    for p, (ns, cols) in POOLS.items():
        for s in range(ns):
            S(f"ws_{p}{s}", [128, cols], BF16)

    for v, n in VEC.items():
        S("c_" + v, [128, n])
    S("ewc", [128, EC])
    S("euc", [128, EC])
    S("ones_bf", [128, 1], BF16)
    S("onesf", [128, 1])
    S("onesrow", [1, 128])
    S("hlast", [128, EC])
    S("h2last", [128, EC])
    S("cnum", [128, EC])
    S("cden", [128, EC])

    S("sx0", [128, EC * NT])
    S("sx1", [128, EC * NT])
    S("sxn", [128, EC * NT], BF16)
    S("hbuf", [128, EC * (NT + 1)], BF16)
    S("xk", [128, EC * NT], BF16)
    S("xv", [128, EC * NT], BF16)
    S("xr", [128, EC * NT], BF16)
    S("att", [128, EC * NT], BF16)
    S("bigbf", [128, FC * NT], BF16)
    for t in ("tr_k0", "tr_k1", "tr_e0", "tr_e1", "tr_r0", "tr_r1", "tr_m"):
        S(t, [128, NT])
    sb["tr_k"] = sb["tr_k0"]
    sb["tr_r"] = sb["tr_r0"]
    sb["tr_e"] = sb["tr_e0"]
    S("snum", [128, NT + 1])
    S("sden", [128, NT + 1])
    S("ssqbf", [128, EC * NT], BF16)
    for r in ("r_mean", "r_var", "r_rstd"):
        S(r, [1, NT])
    S("r_tern", [1, 8])
    S("cst_eps0", [128, 1])
    S("cst_eps", [128, 1])
    S("cst_one", [128, 1])
    nc.const_aps.register(F32, LN0_EPS, sb["cst_eps0"][:])
    nc.const_aps.register(F32, LN_EPS, sb["cst_eps"][:])
    nc.const_aps.register(F32, 1.0, sb["cst_one"][:])
    S("accw1", [128, 16])
    S("accw2", [128, 16])
    S("thr1", [128, 2])
    S("thr2", [128, 2])

    pb = [e(nc.psum_tensor(f"pb{i}", [128, NT], F32)) for i in range(8)]

    semnames = (["T", "V", "S", "G", "CONST", "PB0"]
                + [f"W_{p}{s}" for p, (ns, _) in POOLS.items() for s in range(ns)]
                + [f"PC_{w}" for w in WSPEC]
                + ["X0", "X1", "O"])
    sems = {nm: e(nc.semaphore("sem_" + nm)) for nm in semnames}

    b = Builder()
    bank_rel = {i: [] for i in range(8)}
    slot_rel = {f"{p}{s}": [] for p, (ns, _) in POOLS.items() for s in range(ns)}
    slot_rr = defaultdict(int)

    def wait_all(*lists):
        out = []
        for l in lists:
            out.extend(l)
        return out

    def vec_ap(name, c, w=NT):
        return sb[name][:, c * w:(c + 1) * w]

    for v in VEC:
        b.emit("sync",
               (lambda vv: (lambda s: s.dma_start(out=sb["c_" + vv][:], in_=vdram[vv])))(v),
               inc="CONST", incn=16)
    cwait = [("CONST", b.val("CONST"))]

    b.emit("scalar", lambda s: s.activation(out=sb["ewc"][:], in_=sb["c_td"][:], func=AF.Exp),
           waits=cwait, inc="S")
    b.emit("scalar", lambda s: s.activation(out=sb["ewc"][:], in_=sb["ewc"][:], func=AF.Exp, scale=-1.0),
           waits=[("S", b.val("S"))])
    b.emit("scalar", lambda s: s.activation(out=sb["euc"][:], in_=sb["c_tf"][:], func=AF.Exp), inc="S")
    for nm, val in (("ones_bf", 1.0), ("onesf", 1.0), ("onesrow", 1.0),
                    ("hlast", 0.0), ("h2last", 0.0), ("cnum", 0.0), ("cden", 0.0),
                    ("cst_eps0", LN0_EPS), ("cst_eps", LN_EPS), ("cst_one", 1.0)):
        b.emit("vector", (lambda n_, v_: (lambda v: v.memset(sb[n_][:], v_)))(nm, val))
    b.emit("vector", lambda v: v.engine_nop(), inc="V")
    init_done = [("S", b.val("S")), ("V", b.val("V"))]

    for w in ("wkvr", "wo", "cwr", "cwk", "cwv"):
        b.emit("gpsimd",
               (lambda ww: (lambda g: g.dma_start(
                   out=wscr[ww].rearrange("a b -> (a b)"),
                   in_=wdram[ww].rearrange("a b -> (a b)"))))(w),
               inc="PC_" + w, incn=16)

    def tern_pass_a(w, acc_name):
        mb = NFILLS[w]
        L = FCOLS[w]
        for m in range(mb):
            ld = b.emit("sync",
                        (lambda ww, mm, LL: (lambda s: s.dma_start(
                            out=sb["sx1"][:, 0:LL],
                            in_=wdram[ww][mm * 128:(mm + 1) * 128, :])))(w, m, L),
                        waits=[("V", b.val("V"))], inc="PB0", incn=16)
            b.emit("vector",
                   (lambda mm, LL, aa: (lambda v: v.tensor_reduce(
                       out=sb[aa][:, mm:mm + 1], in_=sb["sx1"][:, 0:LL],
                       axis=AX, op=ALU.add, apply_absolute_value=True)))(m, L, acc_name),
                   waits=[("PB0", ld)], inc="V")

    tern_pass_a("w1t", "accw1")
    tern_pass_a("w2t", "accw2")

    def tern_scale(acc_name, ncols, nelem, thr_name, col):
        b.emit("tensor",
               (lambda an, ncl: (lambda t: t.matmul(
                   pb[0][0:1, 0:ncl], sb["onesf"][:], sb[an][:, 0:ncl],
                   start=True, stop=True)))(acc_name, ncols),
               waits=wait_all([("V", b.val("V"))], bank_rel[0]), inc="T")
        b.emit("vector",
               (lambda ncl, cc: (lambda v: v.tensor_reduce(
                   out=sb["r_tern"][:, cc:cc + 1], in_=pb[0][0:1, 0:ncl],
                   axis=AX, op=ALU.add)))(ncols, col),
               waits=[("T", b.val("T"))], inc="V")
        bank_rel[0] = [("V", b.val("V"))]
        b.emit("vector",
               (lambda cc, ne: (lambda v: v.tensor_scalar_mul(
                   out=sb["r_tern"][:, cc + 1:cc + 2], in0=sb["r_tern"][:, cc:cc + 1],
                   scalar1=1.0 / ne)))(col, nelem),
               waits=[("V", b.val("V"))], inc="V")
        b.emit("vector",
               (lambda cc: (lambda v: v.tensor_scalar_max(
                   out=sb["r_tern"][:, cc + 1:cc + 2], in0=sb["r_tern"][:, cc + 1:cc + 2],
                   scalar1=1e-8)))(col),
               waits=[("V", b.val("V"))], inc="V")
        b.emit("vector",
               (lambda cc: (lambda v: v.tensor_scalar_mul(
                   out=sb["r_tern"][:, cc + 2:cc + 3], in0=sb["r_tern"][:, cc + 1:cc + 2],
                   scalar1=0.5)))(col),
               waits=[("V", b.val("V"))], inc="V")
        b.emit("vector",
               (lambda cc: (lambda v: v.tensor_scalar_mul(
                   out=sb["r_tern"][:, cc + 3:cc + 4], in0=sb["r_tern"][:, cc + 2:cc + 3],
                   scalar1=-1.0)))(col),
               waits=[("V", b.val("V"))], inc="V")
        b.emit("tensor",
               (lambda cc: (lambda t: t.matmul(
                   pb[0][:, 0:2], sb["onesrow"][:], sb["r_tern"][:, cc + 2:cc + 4],
                   start=True, stop=True)))(col),
               waits=wait_all([("V", b.val("V"))], bank_rel[0]), inc="T")
        b.emit("vector",
               (lambda tn: (lambda v: v.tensor_copy(out=sb[tn][:], in_=pb[0][:, 0:2])))(thr_name),
               waits=[("T", b.val("T"))], inc="V")
        bank_rel[0] = [("V", b.val("V"))]

    tern_scale("accw1", 16, 16 * 128 * 2048, "thr1", 0)
    tern_scale("accw2", 8, 8 * 128 * 4096, "thr2", 4)

    def tern_pass_b(w, thr_name):
        mb = NFILLS[w]
        L = FCOLS[w]
        nsub = L // NT
        prev_store = 0
        for m in range(mb):
            ld = b.emit("sync",
                        (lambda ww, mm, LL: (lambda s: s.dma_start(
                            out=sb["sx1"][:, 0:LL],
                            in_=wdram[ww][mm * 128:(mm + 1) * 128, :])))(w, m, L),
                        waits=[("V", b.val("V"))], inc="PB0", incn=16)
            for j in range(nsub):
                sl = slice(j * NT, (j + 1) * NT)
                b.emit("vector",
                       (lambda ss, tn: (lambda v: v.tensor_scalar(
                           out=sb["tr_m"][:], in0=sb["sx1"][:, ss], scalar1=sb[tn][:, 1:2],
                           scalar2=None, op0=ALU.is_le)))(sl, thr_name),
                       waits=wait_all([("PB0", ld)] if j == 0 else [],
                                      [("O", prev_store)] if j == 0 else []))
                b.emit("vector",
                       (lambda ss, tn: (lambda v: v.scalar_tensor_tensor(
                           out=sb["att"][:, ss], in0=sb["sx1"][:, ss], scalar=sb[tn][:, 0:1],
                           in1=sb["tr_m"][:], op0=ALU.is_ge, op1=ALU.subtract)))(sl, thr_name),
                       inc="V")
            prev_store = b.emit(
                "scalar",
                (lambda ww, mm, LL: (lambda s: s.dma_start(
                    out=wscr[ww][mm * 128:(mm + 1) * 128, :], in_=sb["att"][:, 0:LL])))(w, m, L),
                waits=[("V", b.val("V"))], inc="O", incn=16)

    tern_pass_b("w1t", "thr1")
    w1t_done = [("O", b.val("O"))]
    tern_pass_b("w2t", "thr2")
    w2t_done = [("O", b.val("O"))]
    att_free = [("O", b.val("O"))]
    prephase_v = [("V", b.val("V"))]

    FILLROWS = {"wk": 1, "wv": 1, "wr": 1, "wo": 1, "cwr": 1,
                "cwk": 2, "w1t": 2, "cwv": 1, "w2t": 1}

    def wfill(w, fill_idx):
        pool = WPOOL[w]
        ns, cols = POOLS[pool]
        assert cols == FCOLS[w]
        slot = slot_rr[pool] % ns
        slot_rr[pool] += 1
        key = f"{pool}{slot}"
        semn = f"W_{key}"
        src = wscr[w][fill_idx * 128:(fill_idx + 1) * 128, :]
        if w == "w1t":
            pre = w1t_done
        elif w == "w2t":
            pre = w2t_done
        else:
            pre = [("PC_" + w, 16)]
        q = WQUEUE[w]
        cnt = b.emit(q,
                     (lambda ss, kk: (lambda s: s.dma_start(
                         out=sb["ws_" + kk][:], in_=ss)))(src, key),
                     waits=wait_all(slot_rel[key], pre), inc=semn, incn=16)
        return [(semn, cnt)], key

    def wrel(key):
        slot_rel[key] = [("T", b.val("T"))]

    pend = {}

    def wfill_pre(w, idx):
        pend[(w, idx)] = wfill(w, idx)

    def wfill_get(w, idx):
        if (w, idx) in pend:
            return pend.pop((w, idx))
        return wfill(w, idx)

    def mm_acc(bank, lhsT_aps, rhs_aps, extra_waits=(), extra_at=()):
        n = len(lhsT_aps)
        ea = dict(extra_at)
        for i, (lt, rh) in enumerate(zip(lhsT_aps, rhs_aps)):
            w = []
            if i == 0:
                w = wait_all(list(extra_waits), bank_rel[bank])
            if i in ea:
                w = wait_all(w, ea[i])
            b.emit("tensor",
                   (lambda lt_, rh_, st_, sp_: (lambda t: t.matmul(
                       pb[bank][:], lt_, rh_, start=st_, stop=sp_)))(lt, rh, i == 0, i == n - 1),
                   waits=w, inc="T" if i == n - 1 else None)
        return [("T", b.val("T"))]

    def bank_done(bank, sem):
        bank_rel[bank] = [(sem, b.val(sem))]

    statT_chunk = {}

    def ln_stat_mms(ap_fn, first_waits, chunk_waits=None):
        for c in range(EC):
            par = c % 2
            sw = list(first_waits) if c == 0 else []
            if chunk_waits is not None:
                sw = wait_all(sw, chunk_waits(c))
            if c >= 2:
                sw = wait_all(sw, [("T", statT_chunk[c - 2])])
            b.emit("scalar",
                   (lambda aa, pp: (lambda s: s.activation(
                       out=vec_ap("sxbf", pp), in_=aa, func=AF.Copy)))(ap_fn(c), par),
                   waits=sw)
            b.emit("scalar",
                   (lambda aa, pp: (lambda s: s.activation(
                       out=vec_ap("ssqbf", pp), in_=aa, func=AF.Square)))(ap_fn(c), par),
                   inc="S")
            sv = b.val("S")
            w0 = [("S", sv)]
            if c == 0:
                w0 = wait_all(w0, bank_rel[0], bank_rel[1])
            b.emit("tensor",
                   (lambda pp, st_: (lambda t: t.matmul(
                       pb[0][0:1, :], sb["ones_bf"][:], vec_ap("sxbf", pp),
                       start=st_, stop=False)))(par, c == 0),
                   waits=w0)
            b.emit("tensor",
                   (lambda pp, st_, sp_: (lambda t: t.matmul(
                       pb[1][0:1, :], sb["ones_bf"][:], vec_ap("ssqbf", pp),
                       start=st_, stop=sp_)))(par, c == 0, c == EC - 1),
                   inc="T")
            statT_chunk[c] = b.val("T")
        return [("T", b.val("T"))]

    def ln_rows(fused_t1):
        b.emit("vector", lambda v: v.tensor_scalar(
            out=sb["r_mean"][:], in0=pb[4][0:1, :], scalar1=1.0 / E, scalar2=None,
            op0=ALU.mult), waits=[("T", b.val("T"))], inc="V")
        b.emit("vector", lambda v: v.tensor_scalar(
            out=sb["r_var"][:], in0=pb[7][0:1, :], scalar1=1.0 / E, scalar2=None,
            op0=ALU.mult), inc="V")
        b.emit("vector", lambda v: v.tensor_mul(sb["r_rstd"][:], sb["r_mean"][:], sb["r_mean"][:]),
               waits=[("V", b.val("V"))], inc="V")
        b.emit("vector", lambda v: v.tensor_sub(sb["r_var"][:], sb["r_var"][:], sb["r_rstd"][:]),
               waits=[("V", b.val("V"))], inc="V")
        bank_done(4, "V")
        bank_done(7, "V")
        vv = b.val("V")
        eps0 = LN0_EPS if fused_t1 else LN_EPS
        b.emit("scalar", (lambda e0: (lambda s: s.activation(
            out=sb["r_rstd"][:], in_=sb["r_var"][:], func=AF.Ln, bias=e0)))(eps0),
            waits=[("V", vv)], inc="S")
        b.emit("scalar", lambda s: s.activation(
            out=sb["r_rstd"][:], in_=sb["r_rstd"][:], func=AF.Exp, scale=-0.5),
            waits=[("S", b.val("S"))], inc="S")
        sv = b.val("S")
        if fused_t1:
            b.emit("vector", lambda v: v.tensor_mul(sb["r_var"][:], sb["r_var"][:], sb["r_rstd"][:]),
                   waits=[("S", sv)], inc="V")
            b.emit("vector", lambda v: v.tensor_mul(sb["r_var"][:], sb["r_var"][:], sb["r_rstd"][:]),
                   waits=[("V", b.val("V"))], inc="V")
            b.emit("vector", lambda v: v.tensor_mul(sb["r_mean"][:], sb["r_mean"][:], sb["r_rstd"][:]),
                   inc="V")
            vv = b.val("V")
            b.emit("scalar", lambda s: s.activation(
                out=sb["r_var"][:], in_=sb["r_var"][:], func=AF.Ln, bias=LN_EPS),
                waits=[("V", vv)], inc="S")
            b.emit("scalar", lambda s: s.activation(
                out=sb["r_var"][:], in_=sb["r_var"][:], func=AF.Exp, scale=-0.5),
                waits=[("S", b.val("S"))], inc="S")
        else:
            b.emit("vector", lambda v: v.tensor_mul(sb["r_mean"][:], sb["r_mean"][:], sb["r_rstd"][:]),
                   waits=[("S", sv)], inc="V")
        return [("S", b.val("S")), ("V", b.val("V"))]

    def ln_bcasts(rows_ready, banks_rows):
        for idx, (bk, row) in enumerate(banks_rows):
            b.emit("tensor",
                   (lambda bb, rr: (lambda t: t.matmul(
                       pb[bb][:], sb["onesrow"][:], sb[rr][:], start=True, stop=True)))(bk, row),
                   waits=wait_all(rows_ready if idx == 0 else [], bank_rel[bk]),
                   inc="T" if idx == len(banks_rows) - 1 else None)
        return [("T", b.val("T"))]


    ln2gv = {}

    def ln2_sq(c, x2v):
        b.emit("vector",
               (lambda cc: (lambda g: g.tensor_mul(
                   vec_ap("ssqbf", cc), vec_ap("sxn", cc), vec_ap("sxn", cc))))(c),
               waits=wait_all([("V", x2v[c])], fe.get("statT_done", []) if c == 0 else []),
               inc="G")
        ln2gv[c] = b.val("G")

    def ln2_mm_sum(c, x2v):
        w0 = [("V", x2v[c])]
        if c == 0:
            w0 = wait_all(w0, bank_rel[4])
        b.emit("tensor",
               (lambda cc, st_: (lambda t: t.matmul(
                   pb[4][0:1, :], sb["ones_bf"][:], vec_ap("sxn", cc),
                   start=st_, stop=False)))(c, c == 0),
               waits=w0, inc="T")

    def ln2_mm_sq(c):
        w0 = [("G", ln2gv[c])]
        if c == 0:
            w0 = wait_all(w0, bank_rel[7])
        b.emit("tensor",
               (lambda cc, st_, sp_: (lambda t: t.matmul(
                   pb[7][0:1, :], sb["ones_bf"][:], vec_ap("ssqbf", cc),
                   start=st_, stop=sp_)))(c, c == 0, c == EC - 1),
               waits=w0, inc="T")
        statT_chunk[c] = b.val("T")

    outdone = {}
    mix2_done_prev = []
    wkv_busy = []
    qv_hist = {}
    att_hist = {}
    div_hist = {}
    kcsq_busy = []
    fin_busy = []
    gel_busy = []
    hb = sb["hbuf"][:].rearrange("p (c n) -> p c n", c=EC)
    fe = {}

    def emit_xload(it):
        if it >= n_tiles:
            return
        xslot = "sx%d" % (it % 2)
        xsem = "X%d" % (it % 2)
        relw = []
        if it >= 2:
            relw = [("O", outdone[it - 2])]
        if it == 1:
            relw = wait_all(relw, prephase_v)
        cnt = b.emit("gpsimd",
                     (lambda xs, ii: (lambda s: s.dma_start(
                         out=sb[xs][:], in_=xS[ii])))(xslot, it),
                     waits=relw, inc=xsem, incn=16)
        fe["xrdy%d" % it] = [(xsem, cnt)]

    def fe_stat_gp(j, c, first_waits):
        xslot = "sx%d" % (j % 2)
        sw = wait_all(list(first_waits), fe["xrdy%d" % j] if c == 0 else [])
        b.emit("vector",
               (lambda xs, cc: (lambda g: g.tensor_mul(
                   vec_ap("ssqbf", cc), vec_ap(xs, cc), vec_ap(xs, cc))))(xslot, c),
               waits=sw, inc="G")
        fe["gv%d" % c] = b.val("G")

    def fe_stat_mm(j, c):
        xslot = "sx%d" % (j % 2)
        w0 = []
        if c == 0:
            w0 = wait_all(bank_rel[4], bank_rel[7], fe["xrdy%d" % j])
        b.emit("tensor",
               (lambda xs, cc, st_: (lambda t: t.matmul(
                   pb[4][0:1, :], sb["onesf"][:], vec_ap(xs, cc),
                   start=st_, stop=False)))(xslot, c, c == 0),
               waits=w0)
        b.emit("tensor",
               (lambda cc, st_, sp_: (lambda t: t.matmul(
                   pb[7][0:1, :], sb["ones_bf"][:], vec_ap("ssqbf", cc),
                   start=st_, stop=sp_)))(c, c == 0, c == EC - 1),
               waits=[("G", fe["gv%d" % c])], inc="T")
        statT_chunk[c] = b.val("T")

    def fe_rows():
        b.emit("vector", lambda v: v.tensor_scalar(
            out=sb["r_mean"][:], in0=pb[4][0:1, :], scalar1=1.0 / E, scalar2=None,
            op0=ALU.mult), waits=[("T", b.val("T"))], inc="V")
        b.emit("vector", lambda v: v.tensor_scalar(
            out=sb["r_var"][:], in0=pb[7][0:1, :], scalar1=1.0 / E, scalar2=None,
            op0=ALU.mult), inc="V")
        b.emit("vector", lambda v: v.tensor_mul(sb["r_rstd"][:], sb["r_mean"][:], sb["r_mean"][:]),
               waits=[("V", b.val("V"))], inc="V")
        b.emit("vector", lambda v: v.tensor_sub(sb["r_var"][:], sb["r_var"][:], sb["r_rstd"][:]),
               waits=[("V", b.val("V"))], inc="V")
        bank_done(4, "V")
        bank_done(7, "V")
        vv = b.val("V")
        b.emit("scalar", lambda s: s.activation(
            out=sb["r_rstd"][:], in_=sb["r_var"][:], func=AF.Ln, bias=LN0_EPS),
            waits=[("V", vv)], inc="S")
        b.emit("scalar", lambda s: s.activation(
            out=sb["r_rstd"][:], in_=sb["r_rstd"][:], func=AF.Exp, scale=-0.5),
            waits=[("S", b.val("S"))], inc="S")
        sv = b.val("S")
        b.emit("vector", lambda v: v.tensor_mul(sb["r_var"][:], sb["r_var"][:], sb["r_rstd"][:]),
               waits=[("S", sv)], inc="V")
        b.emit("vector", lambda v: v.tensor_mul(sb["r_var"][:], sb["r_var"][:], sb["r_rstd"][:]),
               waits=[("V", b.val("V"))], inc="V")
        b.emit("vector", lambda v: v.tensor_mul(sb["r_mean"][:], sb["r_mean"][:], sb["r_rstd"][:]),
               inc="V")
        vv = b.val("V")
        b.emit("scalar", lambda s: s.activation(
            out=sb["r_var"][:], in_=sb["r_var"][:], func=AF.Ln, bias=LN_EPS),
            waits=[("V", vv)], inc="S")
        b.emit("scalar", lambda s: s.activation(
            out=sb["r_var"][:], in_=sb["r_var"][:], func=AF.Exp, scale=-0.5),
            waits=[("S", b.val("S"))], inc="S")
        return [("S", b.val("S")), ("V", vv)]

    def fe_bcasts(rows_ready):
        return ln_bcasts(rows_ready, [(4, "r_rstd"), (7, "r_mean"), (0, "r_var")])

    def fe_boundary(j):
        b.emit("scalar",
               lambda s: s.copy(out=hb[:, :, 0:1],
                                in_=sb["hlast"][:].rearrange("p (c o) -> p c o", o=1)),
               waits=wait_all(mix2_done_prev, init_done), inc="S")
        fe["hlset"] = b.val("S")

    def fe_bchunk(j, c, bcast_rdy, xk_t_wait, d_t_wait):
        xslot = "sx%d" % (j % 2)
        b.emit("vector",
               (lambda xs, cc: (lambda v: v.tensor_mul(
                   sb["tr_m"][:], vec_ap(xs, cc), pb[4][:])))(xslot, c),
               waits=bcast_rdy if c == 0 else [])
        b.emit("vector",
               (lambda cc: (lambda v: v.tensor_sub(
                   vec_ap("sxn", cc), sb["tr_m"][:], pb[7][:])))(c))
        b.emit("vector",
               (lambda cc: (lambda v: v.tensor_mul(
                   sb["tr_m"][:], vec_ap("sxn", cc), pb[0][:])))(c))
        b.emit("vector",
               (lambda cc: (lambda v: v.tensor_scalar(
                   out=hb[:, cc, 1:NT + 1], in0=sb["tr_m"][:],
                   scalar1=sb["c_g1"][:, cc:cc + 1], scalar2=sb["c_b1v"][:, cc:cc + 1],
                   op0=ALU.mult, op1=ALU.add)))(c),
               waits=[("S", fe["hlset"])] if c == 0 else [], inc="V")
        dw = wait_all(xk_t_wait if c == 0 else [], d_t_wait if c == 0 else [])
        b.emit("vector",
               (lambda cc: (lambda v: v.tensor_sub(
                   sb["tr_k"][:], hb[:, cc, 1:NT + 1], hb[:, cc, 0:NT])))(c),
               waits=dw)
        for mname, dst in (("mk", "xk"), ("mv", "xv"), ("mr", "xr")):
            b.emit("vector",
                   (lambda cc, mn, dd: (lambda v: v.scalar_tensor_tensor(
                       out=vec_ap(dd, cc), in0=sb["tr_k"][:],
                       scalar=sb["c_" + mn][:, cc:cc + 1], in1=hb[:, cc, 0:NT],
                       op0=ALU.mult, op1=ALU.add)))(c, mname, dst),
                   inc="V" if dst == "xr" else None)

    def fe_finish(j):
        bank_done(4, "V")
        bank_done(7, "V")
        bank_done(0, "V")
        fe["mixes%d" % j] = [("V", b.val("V"))]
        fe["xnbf%d" % j] = [("V", b.val("V"))]
        b.emit("scalar",
               lambda s: s.copy(out=sb["hlast"][:].rearrange("p (c o) -> p c o", o=1),
                                in_=hb[:, :, NT:NT + 1]),
               waits=fe["mixes%d" % j], inc="S")

    def emit_div(gg):
        pqk = "tr_k%d" % (gg % 2)
        pqm = "tr_r%d" % (gg % 2)
        b.emit("scalar",
               (lambda pk_: (lambda s: s.activation(out=sb[pk_][:], in_=sb[pk_][:], func=AF.Ln)))(pqk),
               waits=[("V", qv_hist[gg])])
        b.emit("scalar",
               (lambda pm, pk_: (lambda s: s.activation(
                   out=sb[pm][:], in_=sb[pk_][:], func=AF.Exp, scale=-1.0)))(pqm, pqk),
               inc="S")
        div_hist[gg] = b.val("S")

    def emit_att(gg):
        pm = "tr_r%d" % (gg % 2)
        pe = "tr_e%d" % (gg % 2)
        aw = [("S", div_hist[gg])]
        if aw_att_free.get(gg):
            aw = wait_all(aw, aw_att_free[gg])
        b.emit("vector",
               (lambda g_, pm_, pe_: (lambda v: v.tensor_mul(
                   vec_ap("att", g_), sb[pe_][:], sb[pm_][:])))(gg, pm, pe),
               waits=aw, inc="V")
        att_hist[gg] = b.val("V")

    aw_att_free = {}

    emit_xload(0)
    emit_xload(1)
    for c in range(EC):
        fe_stat_gp(0, c, init_done if c == 0 else [])
    for c in range(EC):
        fe_stat_mm(0, c)
    fe["statT_done"] = [("T", b.val("T"))]
    rows_rdy = fe_rows()
    bc_rdy = fe_bcasts(rows_rdy)
    fe_boundary(0)
    for c in range(EC):
        fe_bchunk(0, c, bc_rdy, att_free, [])
    fe_finish(0)

    for it in range(n_tiles):
        xslot = "sx%d" % (it % 2)
        mixes_done = fe["mixes%d" % it]
        xnbf_all = fe["xnbf%d" % it]

        aw_att_free[0] = att_free if it == 0 else None
        for g in range(EC):
            if g == 2:
                wfill_pre("w1t", 0)
                wfill_pre("w1t", 1)
            if g == 4:
                wfill_pre("w1t", 2)
            kb, vb_, rb = (5, 6, 7) if g % 2 == 0 else (2, 3, 4)
            kp = "tr_k%d" % (g % 2)
            rp = "tr_r%d" % (g % 2)
            ep = "tr_e%d" % (g % 2)
            wk_rdy, wk_key = wfill_get("wkvr", g)
            lk = [sb["ws_" + wk_key][:, kc * 128:(kc + 1) * 128] for kc in range(EC)]
            lv = [sb["ws_" + wk_key][:, 1024 + kc * 128:1024 + (kc + 1) * 128] for kc in range(EC)]
            lr = [sb["ws_" + wk_key][:, 2048 + kc * 128:2048 + (kc + 1) * 128] for kc in range(EC)]
            rk = [vec_ap("xk", kc) for kc in range(EC)]
            rv = [vec_ap("xv", kc) for kc in range(EC)]
            rr = [vec_ap("xr", kc) for kc in range(EC)]
            kT = mm_acc(kb, lk, rk, extra_waits=wait_all(wk_rdy, mixes_done if g == 0 else []))
            b.emit("scalar",
                   (lambda bb, pp: (lambda s: s.activation(out=sb[pp][:], in_=pb[bb][:], func=AF.Exp)))(kb, kp),
                   waits=wait_all(kT, fin_busy if g == 0 else []))
            b.emit("scalar",
                   (lambda gg: (lambda s: s.copy(out=sb["snum"][:, 0:1], in_=sb["cnum"][:, gg:gg + 1])))(g),
                   waits=[("V", qv_hist[(g - 1) % EC])] if qv_hist.get((g - 1) % EC) else [])
            b.emit("scalar",
                   (lambda gg: (lambda s: s.copy(out=sb["sden"][:, 0:1], in_=sb["cden"][:, gg:gg + 1])))(g),
                   inc="S")
            ev_done = b.val("S")
            bank_done(kb, "S")
            vT = mm_acc(vb_, lv, rv)
            rT = mm_acc(rb, lr, rr)
            wrel(wk_key)
            b.emit("scalar",
                   (lambda bb, pp: (lambda s: s.activation(
                       out=sb[pp][:], in_=pb[bb][:], func=AF.Exp, scale=-1.0)))(rb, rp),
                   waits=wait_all(rT, [("V", att_hist[(g - 2) % EC])] if att_hist.get((g - 2) % EC) else []),
                   inc="S")
            bank_done(rb, "S")
            if g >= 1:
                emit_div(g - 1)

            ew_b, _ = bass.broadcast_tensor_aps(sb["ewc"][:, g:g + 1], sb[kp][:])
            b.emit("vector",
                   (lambda bb, pp: (lambda v: v.tensor_mul(sb["tr_m"][:], sb[pp][:], pb[bb][:])))(vb_, kp),
                   waits=wait_all([("S", ev_done)], vT), inc="V")
            bank_done(vb_, "V")
            b.emit("vector",
                   (lambda gg, pe: (lambda v: v.tensor_scalar(
                       out=sb[pe][:], in0=sb["tr_m"][:], scalar1=sb["euc"][:, gg:gg + 1],
                       scalar2=None, op0=ALU.mult)))(g, ep))
            if g >= 1:
                emit_att(g - 1)
            b.emit("vector",
                   (lambda gg, ewb: (lambda v: v.tensor_tensor_scan(
                       out=sb["snum"][:, 1:NT + 1], data0=ewb, data1=sb["tr_m"][:],
                       initial=sb["cnum"][:, gg:gg + 1], op0=ALU.mult, op1=ALU.add)))(g, ew_b))
            b.emit("vector",
                   (lambda gg, ewb, pp: (lambda v: v.tensor_tensor_scan(
                       out=sb["sden"][:, 1:NT + 1], data0=ewb, data1=sb[pp][:],
                       initial=sb["cden"][:, gg:gg + 1], op0=ALU.mult, op1=ALU.add)))(g, ew_b, kp))
            b.emit("vector",
                   (lambda pe: (lambda v: v.tensor_add(sb[pe][:], sb[pe][:], sb["snum"][:, 0:NT])))(ep),
                   waits=[("S", ev_done)])
            b.emit("vector",
                   (lambda gg, pp: (lambda v: v.scalar_tensor_tensor(
                       out=sb[pp][:], in0=sb[pp][:], scalar=sb["euc"][:, gg:gg + 1],
                       in1=sb["sden"][:, 0:NT], op0=ALU.mult, op1=ALU.add)))(g, kp))
            b.emit("vector",
                   (lambda pp, pr: (lambda v: v.scalar_tensor_tensor(
                       out=sb[pp][:], in0=sb[pr][:], scalar=1.0, in1=sb[pp][:],
                       op0=ALU.add, op1=ALU.mult)))(kp, rp), inc="V")
            qv_hist[g] = b.val("V")
            b.emit("scalar",
                   (lambda gg: (lambda s: s.copy(out=sb["cnum"][:, gg:gg + 1], in_=sb["snum"][:, NT:NT + 1])))(g),
                   waits=[("V", qv_hist[g])])
            b.emit("scalar",
                   (lambda gg: (lambda s: s.copy(out=sb["cden"][:, gg:gg + 1], in_=sb["sden"][:, NT:NT + 1])))(g),
                   inc="S")
        emit_div(EC - 1)
        emit_att(EC - 1)
        att_done = [("V", b.val("V"))]

        for f2 in range(FC // 2):
            w1_rdy, w1_key = wfill_get("w1t", f2)
            if f2 + 3 < FC // 2 and f2 >= 0:
                pass
            for half in range(2):
                f = 2 * f2 + half
                hbk = 0 if f % 2 == 0 else 1
                lw = [sb["ws_" + w1_key][:, half * 1024 + kc * 128: half * 1024 + (kc + 1) * 128]
                      for kc in range(EC)]
                rx = [vec_ap("sxn", kc) for kc in range(EC)]
                hT = mm_acc(hbk, lw, rx,
                            extra_waits=wait_all(w1_rdy if half == 0 else [],
                                                 xnbf_all if f == 0 else []))
                b.emit("scalar",
                       (lambda ff, bb: (lambda s: s.activation(
                           out=vec_ap("bigbf", ff), in_=pb[bb][:], func=AF.Gelu,
                           bias=sb["c_b1f"][:, ff:ff + 1])))(f, hbk),
                       waits=wait_all(hT, gel_busy), inc="S")
                bank_done(hbk, "S")
                gel_busy = [("S", b.val("S"))]
            wrel(w1_key)
        gel_done = [("S", b.val("S"))]

        nxt = it + 1
        have_next = nxt < n_tiles
        wfill_pre("w2t", 0)
        wfill_pre("w2t", 1)
        x2v = {}
        for c2 in range(EC // 2):
            wo_rdy, wo_key = wfill("wo", c2)
            for half in range(2):
                c = 2 * c2 + half
                yb = 5 if c % 2 == 0 else 6
                lo = [sb["ws_" + wo_key][:, half * 1024 + kc * 128: half * 1024 + (kc + 1) * 128]
                      for kc in range(EC)]
                ro = [vec_ap("att", kc) for kc in range(EC)]
                yT = mm_acc(yb, lo, ro,
                            extra_waits=wait_all(wo_rdy if half == 0 else [],
                                                 att_done if c == 0 else []))
                b.emit("vector",
                       (lambda cc, bb: (lambda v: v.tensor_add(
                           vec_ap("sxn", cc), vec_ap("sxn", cc), pb[bb][:])))(c, yb),
                       waits=yT, inc="V")
                bank_done(yb, "V")
                x2v[c] = b.val("V")
                ln2_sq(c, x2v)
                if c >= 1:
                    ln2_mm_sum(c - 1, x2v)
                if c >= 3:
                    ln2_mm_sq(c - 3)
            wrel(wo_key)

        wfill_pre("cwr", 0)
        wfill_pre("cwr", 1)
        b.emit("scalar",
               lambda s: s.copy(out=hb[:, :, 0:1],
                                in_=sb["h2last"][:].rearrange("p (c o) -> p c o", o=1)),
               waits=mixes_done, inc="S")
        h2l_set = b.val("S")
        bcast2 = None
        rows2 = None

        def h2_chunk(cc):
            b.emit("vector",
                   (lambda c_: (lambda v: v.tensor_sub(sb["tr_m"][:], vec_ap("sxn", c_), pb[3][:])))(cc),
                   waits=bcast2 if cc == 0 else [])
            b.emit("vector",
                   lambda v: v.tensor_mul(sb["tr_m"][:], sb["tr_m"][:], pb[2][:]))
            b.emit("vector",
                   (lambda c_: (lambda v: v.tensor_scalar(
                       out=hb[:, c_, 1:NT + 1], in0=sb["tr_m"][:],
                       scalar1=sb["c_g2"][:, c_:c_ + 1], scalar2=sb["c_b2v"][:, c_:c_ + 1],
                       op0=ALU.mult, op1=ALU.add)))(cc),
                   waits=[("S", h2l_set)] if cc == 0 else [], inc="V")
            b.emit("vector",
                   (lambda c_: (lambda v: v.tensor_sub(
                       sb["tr_k"][:], hb[:, c_, 1:NT + 1], hb[:, c_, 0:NT])))(cc))
            for mname, dst in (("cmk", "xk"), ("cmr", "xr")):
                b.emit("vector",
                       (lambda c_, mn, dd: (lambda v: v.scalar_tensor_tensor(
                           out=vec_ap(dd, c_), in0=sb["tr_k"][:],
                           scalar=sb["c_" + mn][:, c_:c_ + 1], in1=hb[:, c_, 0:NT],
                           op0=ALU.mult, op1=ALU.add)))(cc, mname, dst),
                       inc="V" if dst == "xr" else None)

        for c in range(EC):
            ob = 0 if c % 2 == 0 else 1
            h0_rdy, h0_key = wfill_get("w2t", c)
            lhs = [sb["ws_" + h0_key][:, kk * 128:(kk + 1) * 128] for kk in range(32)]
            rhs = [vec_ap("bigbf", kk) for kk in range(32)]
            oT = mm_acc(ob, lhs, rhs,
                        extra_waits=wait_all(h0_rdy, gel_done if c == 0 else []))
            wrel(h0_key)
            if c == 0:
                ln2_mm_sum(7, x2v)
                ln2_mm_sq(5)
                ln2_mm_sq(6)
                ln2_mm_sq(7)
            elif c == 1:
                rows2 = ln_rows(False)
                if have_next:
                    for cc in range(EC):
                        fe_stat_gp(nxt, cc, [("T", statT_chunk[cc])])
            elif c == 2:
                bcast2 = ln_bcasts(rows2, [(2, "r_rstd"), (3, "r_mean")])
            b.emit("vector",
                   (lambda cc, bb: (lambda v: v.scalar_tensor_tensor(
                       out=sb["tr_m"][:], in0=pb[bb][:], scalar=sb["c_b2f"][:, cc:cc + 1],
                       in1=vec_ap("sxn", cc), op0=ALU.add, op1=ALU.add)))(c, ob),
                   waits=oT)
            b.emit("vector",
                   (lambda cc, xs: (lambda v: v.tensor_add(
                       vec_ap(xs, cc), vec_ap(xs, cc), sb["tr_m"][:])))(c, xslot),
                   inc="V")
            bank_done(ob, "V")
            if c >= 2:
                h2_chunk(c - 2)
        for cc in range(6, EC):
            h2_chunk(cc)
        bank_done(2, "V")
        bank_done(3, "V")
        mix2_done = [("V", b.val("V"))]
        mix2_done_prev = mix2_done
        b.emit("scalar",
               lambda s: s.copy(out=sb["h2last"][:].rearrange("p (c o) -> p c o", o=1),
                                in_=hb[:, :, NT:NT + 1]),
               waits=mix2_done, inc="S")
        wfill_pre("cwk", 0)
        wfill_pre("cwk", 1)

        for f2 in range(FC // 2):
            cwk_rdy, cwk_key = wfill_get("cwk", f2)
            for half in range(2):
                f = 2 * f2 + half
                kbk = 0 if f % 2 == 0 else 1
                lw = [sb["ws_" + cwk_key][:, half * 1024 + kc * 128: half * 1024 + (kc + 1) * 128]
                      for kc in range(EC)]
                rk = [vec_ap("xk", kc) for kc in range(EC)]
                kTd = mm_acc(kbk, lw, rk,
                             extra_waits=wait_all(cwk_rdy if half == 0 else [],
                                                  mix2_done if f == 0 else []))
                b.emit("vector",
                       (lambda bb: (lambda v: v.tensor_scalar(
                           out=sb["tr_k"][:], in0=pb[bb][:], scalar1=0.0, scalar2=None,
                           op0=ALU.max)))(kbk),
                       waits=wait_all(kTd, kcsq_busy), inc="V")
                bank_done(kbk, "V")
                b.emit("scalar",
                       (lambda ff: (lambda s: s.activation(
                           out=vec_ap("bigbf", ff), in_=sb["tr_k"][:], func=AF.Square)))(f),
                       waits=[("V", b.val("V"))], inc="S")
                kcsq_busy = [("S", b.val("S"))]
            wrel(cwk_key)
            if have_next and 4 <= f2 <= 11:
                fe_stat_mm(nxt, f2 - 4)
        kc_done = [("S", b.val("S"))]
        if have_next:
            wfill_pre("wkvr", 0)
            wfill_pre("wkvr", 1)
            fe["statT_done"] = [("T", b.val("T"))]
            fe["rows_rdy"] = fe_rows()

        wfill_pre("cwv", 0)
        wfill_pre("cwv", 1)
        for c in range(EC):
            rbk = 2 if c % 2 == 0 else 3
            if c % 2 == 0:
                cwr_rdy, cwr_key = wfill_get("cwr", c // 2)
            half = c % 2
            lr = [sb["ws_" + cwr_key][:, half * 1024 + kc * 128: half * 1024 + (kc + 1) * 128]
                  for kc in range(EC)]
            rr = [vec_ap("xr", kc) for kc in range(EC)]
            rTd = mm_acc(rbk, lr, rr,
                         extra_waits=wait_all(cwr_rdy if half == 0 else [],
                                              att_done if c == 0 else []))
            if half == 1:
                wrel(cwr_key)
            b.emit("scalar",
                   (lambda bb: (lambda s: s.activation(
                       out=sb["tr_r"][:], in_=pb[bb][:], func=AF.Exp, scale=-1.0)))(rbk),
                   waits=wait_all(rTd, fin_busy if c == 0 else [],
                                  att_free if (it == 0 and c == 0) else []))
            b.emit("scalar", lambda s: s.activation(
                out=sb["tr_r"][:], in_=sb["tr_r"][:], func=AF.Ln, bias=1.0))
            b.emit("scalar",
                   (lambda cc: (lambda s: s.activation(
                       out=vec_ap("att", cc), in_=sb["tr_r"][:], func=AF.Exp, scale=-1.0)))(c),
                   inc="S")
            bank_done(rbk, "S")
        rq_done = [("S", b.val("S"))]
        cwr_T_done = [("T", b.val("T"))]
        if have_next:
            fe["bc_rdy"] = fe_bcasts(fe["rows_rdy"])

        if have_next:
            fe_boundary(nxt)
        for c in range(EC):
            ob = 5 if c % 2 == 0 else 6
            h0_rdy, h0_key = wfill_get("cwv", c)
            lhs = [sb["ws_" + h0_key][:, kk * 128:(kk + 1) * 128] for kk in range(32)]
            rhs = [vec_ap("bigbf", kk) for kk in range(32)]
            kcvT = mm_acc(ob, lhs, rhs,
                          extra_waits=wait_all(h0_rdy, kc_done if c == 0 else []))
            wrel(h0_key)
            b.emit("vector",
                   (lambda bb, cc: (lambda v: v.tensor_mul(
                       sb["tr_e"][:], pb[bb][:], vec_ap("att", cc))))(ob, c),
                   waits=wait_all(kcvT, rq_done if c == 0 else []))
            b.emit("vector",
                   (lambda cc, xs: (lambda v: v.tensor_add(
                       vec_ap(xs, cc), vec_ap(xs, cc), sb["tr_e"][:])))(c, xslot),
                   inc="V")
            bank_done(ob, "V")
            fin_busy = [("V", b.val("V"))]
            if have_next:
                fe_bchunk(nxt, c, fe["bc_rdy"],
                          cwr_T_done,
                          [("T", statT_chunk[EC - 1])])
        fin_all = [("V", b.val("V"))]
        if have_next:
            fe_finish(nxt)

        outdone[it] = b.emit("gpsimd",
                             (lambda xs, ii: (lambda s: s.dma_start(
                                 out=outS[ii], in_=sb[xs][:])))(xslot, it),
                             waits=fin_all, inc="O", incn=16)
        emit_xload(it + 2)

    b.emit("gpsimd", lambda g: g.engine_nop(), waits=[("O", b.val("O"))])

    with nc.Block() as block:
        def realize(eng_name):
            def body(eng):
                for ws, fn, inc, incn in b.ev[eng_name]:
                    for s, v in ws:
                        eng.wait_ge(sems[s], v)
                    instr = fn(eng) if fn is not None else None
                    if inc is not None:
                        if instr is not None:
                            instr.then_inc(sems[inc], incn)
                        else:
                            eng.sem_inc(sems[inc], incn)
            return body

        block.tensor(realize("tensor"))
        block.vector(realize("vector"))
        block.scalar(realize("scalar"))
        block.sync(realize("sync"))
        block.gpsimd(realize("gpsimd"))

    ctx.close()
    return nc


def _stream_layout(w, kc, mb, pair=False, halfk=False):
    K, M = w.shape
    assert K == kc * 128 and M == mb * 128
    a = w.reshape(kc, 128, mb, 128)
    if halfk:
        h = kc // 2
        a = a.reshape(2, h, 128, mb, 128)
        out = a.transpose(3, 0, 2, 1, 4)
        return np.ascontiguousarray(out.reshape(mb * 2 * 128, h * 128))
    if pair:
        a = a.reshape(kc, 128, mb // 2, 2, 128)
        out = a.transpose(2, 1, 3, 0, 4)
        return np.ascontiguousarray(out.reshape((mb // 2) * 128, 2 * kc * 128))
    out = a.transpose(2, 1, 0, 3)
    return np.ascontiguousarray(out.reshape(mb * 128, kc * 128))


def _vec_layout(v):
    n = v.shape[0] // 128
    return np.ascontiguousarray(v.reshape(n, 128).T)


def _x_stream(x2d, t_total=T_TOTAL):
    n_tiles = t_total // NT
    a = x2d.T.reshape(EC, 128, n_tiles, NT)
    return np.ascontiguousarray(a.transpose(2, 1, 0, 3).reshape(n_tiles, 128, EC * NT))


def _x_unstream(s, t_total=T_TOTAL):
    n_tiles = t_total // NT
    a = s.reshape(n_tiles, 128, EC, NT).transpose(2, 1, 0, 3)
    return np.ascontiguousarray(a.reshape(E, t_total).T)


_prog_cache = {}


def _prep_weights(inp):
    f32 = lambda a: np.asarray(a, dtype=np.float32)
    wk = _stream_layout(f32(inp["L0_attn_tm_Wk"]), 8, 8)
    wv = _stream_layout(f32(inp["L0_attn_tm_Wv"]), 8, 8)
    wr = _stream_layout(f32(inp["L0_attn_tm_Wr"]), 8, 8)
    return {
        "wkvr": np.ascontiguousarray(np.concatenate([wk, wv, wr], axis=1)),
        "wo": _stream_layout(f32(inp["L0_attn_Wo"]), 8, 8, pair=True),
        "cwr": _stream_layout(f32(inp["L0_attn_cm_Wr"]), 8, 8, pair=True),
        "cwk": _stream_layout(f32(inp["L0_attn_cm_Wk"]), 8, 32, pair=True),
        "w1t": _stream_layout(np.ascontiguousarray(f32(inp["L0_ffn_W1"]).T), 8, 32, pair=True),
        "cwv": _stream_layout(f32(inp["L0_attn_cm_Wv"]), 32, 8),
        "w2t": _stream_layout(np.ascontiguousarray(f32(inp["L0_ffn_W2"]).T), 32, 8),
    }


def _prep_vecs(inp):
    f32 = lambda a: np.asarray(a, dtype=np.float32)
    return {
        "g1": _vec_layout(f32(inp["L0_attn_ln1_g"])), "b1v": _vec_layout(f32(inp["L0_attn_ln1_b"])),
        "g2": _vec_layout(f32(inp["L0_attn_ln2_g"])), "b2v": _vec_layout(f32(inp["L0_attn_ln2_b"])),
        "mk": _vec_layout(f32(inp["L0_attn_tm_mix_k"])), "mv": _vec_layout(f32(inp["L0_attn_tm_mix_v"])),
        "mr": _vec_layout(f32(inp["L0_attn_tm_mix_r"])), "cmk": _vec_layout(f32(inp["L0_attn_cm_mix_k"])),
        "cmr": _vec_layout(f32(inp["L0_attn_cm_mix_r"])), "td": _vec_layout(f32(inp["L0_attn_time_decay"])),
        "tf": _vec_layout(f32(inp["L0_attn_time_first"])), "b2f": _vec_layout(f32(inp["L0_ffn_b2"])),
        "b1f": _vec_layout(f32(inp["L0_ffn_b1"])),
    }


def kernel(
    L0_attn_inputs, L0_attn_ln1_g, L0_attn_ln1_b, L0_attn_ln2_g, L0_attn_ln2_b,
    L0_attn_tm_mix_k, L0_attn_tm_mix_v, L0_attn_tm_mix_r, L0_attn_tm_Wk,
    L0_attn_tm_Wv, L0_attn_tm_Wr, L0_attn_time_decay, L0_attn_time_first,
    L0_attn_Wo, L0_attn_cm_mix_k, L0_attn_cm_mix_r, L0_attn_cm_Wk,
    L0_attn_cm_Wv, L0_attn_cm_Wr, L0_ffn_W1, L0_ffn_b1, L0_ffn_W2, L0_ffn_b2,
):
    x = np.asarray(L0_attn_inputs, dtype=np.float32)
    B, T, E_ = x.shape
    assert (T, E_) == (T_TOTAL, E) and B == N_CORES

    inp = dict(
        L0_attn_tm_Wk=L0_attn_tm_Wk, L0_attn_tm_Wv=L0_attn_tm_Wv,
        L0_attn_tm_Wr=L0_attn_tm_Wr, L0_attn_Wo=L0_attn_Wo,
        L0_attn_cm_Wr=L0_attn_cm_Wr, L0_attn_cm_Wk=L0_attn_cm_Wk,
        L0_attn_cm_Wv=L0_attn_cm_Wv, L0_ffn_W1=L0_ffn_W1, L0_ffn_W2=L0_ffn_W2,
        L0_attn_ln1_g=L0_attn_ln1_g, L0_attn_ln1_b=L0_attn_ln1_b,
        L0_attn_ln2_g=L0_attn_ln2_g, L0_attn_ln2_b=L0_attn_ln2_b,
        L0_attn_tm_mix_k=L0_attn_tm_mix_k, L0_attn_tm_mix_v=L0_attn_tm_mix_v,
        L0_attn_tm_mix_r=L0_attn_tm_mix_r, L0_attn_cm_mix_k=L0_attn_cm_mix_k,
        L0_attn_cm_mix_r=L0_attn_cm_mix_r, L0_attn_time_decay=L0_attn_time_decay,
        L0_attn_time_first=L0_attn_time_first, L0_ffn_b1=L0_ffn_b1,
        L0_ffn_b2=L0_ffn_b2,
    )
    wmats = _prep_weights(inp)
    vecs = _prep_vecs(inp)

    if "nc" not in _prog_cache:
        _prog_cache["nc"] = build_program()
    nc = _prog_cache["nc"]

    in_maps = []
    for bI in range(N_CORES):
        m = {"xS": _x_stream(x[bI])}
        m.update(wmats)
        m.update(vecs)
        in_maps.append(m)

    res = run_bass_kernel_spmd(nc, in_maps, core_ids=list(range(N_CORES)))
    out = np.stack([_x_unstream(np.asarray(r["outS"])) for r in res.results], axis=0)
    return out.astype(np.float32)


# version 11
# speedup vs baseline: 1.0496x; 1.0496x over previous
import numpy as np
from collections import defaultdict
from contextlib import ExitStack

import concourse.bass as bass
import concourse.mybir as mybir
from concourse.bass_utils import run_bass_kernel_spmd

F32 = mybir.dt.float32
BF16 = mybir.dt.bfloat16
AF = mybir.ActivationFunctionType
ALU = mybir.AluOpType
AX = mybir.AxisListType.X

E = 1024
FF = 4096
EC = E // 128
FC = FF // 128
NT = 512
T_TOTAL = 8192
N_CORES = 8

LN0_EPS = 1e-5
LN_EPS = 1e-3
ENGS = ("tensor", "vector", "scalar", "sync", "gpsimd")

WSPEC = ("wkvr", "wo", "cwr", "cwk", "w1t", "cwv", "w2t")
POOLS = {"pkvr": (2, 3072),
         "po": (2, 2048),
         "pk": (3, 2048),
         "pv": (2, 4096)}
WPOOL = {"wkvr": "pkvr", "wo": "po", "cwr": "po",
         "cwk": "pk", "w1t": "pk", "cwv": "pv", "w2t": "pv"}
WQUEUE = {"wkvr": "sync", "wo": "sync", "cwr": "sync",
          "cwk": "sync", "w1t": "sync", "cwv": "sync", "w2t": "sync"}


class Builder:
    def __init__(self):
        self.ev = {k: [] for k in ENGS}
        self.ctr = defaultdict(int)
        self.waited = {k: defaultdict(int) for k in ENGS}

    def emit(self, eng, fn, waits=(), inc=None, incn=1):
        ws = []
        for s, v in waits:
            if v > self.waited[eng][s]:
                self.waited[eng][s] = v
                ws.append((s, v))
        if inc is not None:
            self.ctr[inc] += incn
        self.ev[eng].append((ws, fn, inc, incn))
        return self.ctr[inc] if inc is not None else None

    def val(self, sem):
        return self.ctr[sem]


def build_program(t_total=T_TOTAL, dbg=False):
    n_tiles = t_total // NT
    nc = bass.Bass("TRN2", target_bir_lowering=False)

    xS = nc.dram_tensor("xS", [n_tiles, 128, EC * NT], F32, kind="ExternalInput").ap()
    outS = nc.dram_tensor("outS", [n_tiles, 128, EC * NT], F32, kind="ExternalOutput").ap()

    NFILLS = {"wkvr": 8, "wo": 4, "cwr": 4,
              "cwk": 16, "w1t": 16, "cwv": 8, "w2t": 8}
    FCOLS = {"wkvr": 3072, "wo": 2048, "cwr": 2048,
             "cwk": 2048, "w1t": 2048, "cwv": 4096, "w2t": 4096}
    wdram, wscr = {}, {}
    for w in WSPEC:
        wdram[w] = nc.dram_tensor(w, [NFILLS[w] * 128, FCOLS[w]], F32, kind="ExternalInput").ap()
        wscr[w] = nc.dram_tensor("z" + w, [NFILLS[w] * 128, FCOLS[w]], BF16, kind="Internal").ap()

    VEC = {"g1": EC, "b1v": EC, "g2": EC, "b2v": EC, "mk": EC, "mv": EC,
           "mr": EC, "cmk": EC, "cmr": EC, "td": EC, "tf": EC, "b2f": EC,
           "b1f": FC}
    vdram = {v: nc.dram_tensor(v, [128, n], F32, kind="ExternalInput").ap()
             for v, n in VEC.items()}

    ctx = ExitStack()
    e = ctx.enter_context
    sb = {}

    def S(name, shape, dt=F32):
        sb[name] = e(nc.sbuf_tensor(name, shape, dt))
        return sb[name]

    for p, (ns, cols) in POOLS.items():
        for s in range(ns):
            S(f"ws_{p}{s}", [128, cols], BF16)

    for v, n in VEC.items():
        S("c_" + v, [128, n])
    S("ewc", [128, EC])
    S("euc", [128, EC])
    S("ones_bf", [128, 1], BF16)
    S("onesf", [128, 1])
    S("onesrow", [1, 128])
    S("hlast", [128, EC])
    S("h2last", [128, EC])
    S("cnum", [128, EC])
    S("cden", [128, EC])

    S("sx0", [128, EC * NT])
    S("sx1", [128, EC * NT])
    S("sxn", [128, EC * NT], BF16)
    S("hbuf", [128, EC * (NT + 1)], BF16)
    S("xk", [128, EC * NT], BF16)
    S("xv", [128, EC * NT], BF16)
    S("xr", [128, EC * NT], BF16)
    S("att", [128, EC * NT], BF16)
    S("bigbf", [128, FC * NT], BF16)
    for t in ("tr_k0", "tr_k1", "tr_e0", "tr_e1", "tr_r0", "tr_r1", "tr_m"):
        S(t, [128, NT])
    sb["tr_k"] = sb["tr_k0"]
    sb["tr_r"] = sb["tr_r0"]
    sb["tr_e"] = sb["tr_e0"]
    S("snum", [128, NT + 1])
    S("sden", [128, NT + 1])
    S("ssqbf", [128, EC * NT], BF16)
    for r in ("r_mean", "r_var", "r_rstd"):
        S(r, [1, NT])
    S("r_tern", [1, 8])
    S("cst_eps0", [128, 1])
    S("cst_eps", [128, 1])
    S("cst_one", [128, 1])
    nc.const_aps.register(F32, LN0_EPS, sb["cst_eps0"][:])
    nc.const_aps.register(F32, LN_EPS, sb["cst_eps"][:])
    nc.const_aps.register(F32, 1.0, sb["cst_one"][:])
    S("accw1", [128, 16])
    S("accw2", [128, 16])
    S("thr1", [128, 2])
    S("thr2", [128, 2])

    pb = [e(nc.psum_tensor(f"pb{i}", [128, NT], F32)) for i in range(8)]

    semnames = (["T", "V", "S", "G", "CONST", "PB0"]
                + [f"W_{p}{s}" for p, (ns, _) in POOLS.items() for s in range(ns)]
                + [f"PC_{w}" for w in WSPEC]
                + ["X0", "X1", "O"])
    sems = {nm: e(nc.semaphore("sem_" + nm)) for nm in semnames}

    b = Builder()
    bank_rel = {i: [] for i in range(8)}
    slot_rel = {f"{p}{s}": [] for p, (ns, _) in POOLS.items() for s in range(ns)}
    slot_rr = defaultdict(int)

    def wait_all(*lists):
        out = []
        for l in lists:
            out.extend(l)
        return out

    def vec_ap(name, c, w=NT):
        return sb[name][:, c * w:(c + 1) * w]

    for v in VEC:
        b.emit("sync",
               (lambda vv: (lambda s: s.dma_start(out=sb["c_" + vv][:], in_=vdram[vv])))(v),
               inc="CONST", incn=16)
    cwait = [("CONST", b.val("CONST"))]

    b.emit("scalar", lambda s: s.activation(out=sb["ewc"][:], in_=sb["c_td"][:], func=AF.Exp),
           waits=cwait, inc="S")
    b.emit("scalar", lambda s: s.activation(out=sb["ewc"][:], in_=sb["ewc"][:], func=AF.Exp, scale=-1.0),
           waits=[("S", b.val("S"))])
    b.emit("scalar", lambda s: s.activation(out=sb["euc"][:], in_=sb["c_tf"][:], func=AF.Exp), inc="S")
    for nm, val in (("ones_bf", 1.0), ("onesf", 1.0), ("onesrow", 1.0),
                    ("hlast", 0.0), ("h2last", 0.0), ("cnum", 0.0), ("cden", 0.0),
                    ("cst_eps0", LN0_EPS), ("cst_eps", LN_EPS), ("cst_one", 1.0)):
        b.emit("vector", (lambda n_, v_: (lambda v: v.memset(sb[n_][:], v_)))(nm, val))
    b.emit("vector", lambda v: v.engine_nop(), inc="V")
    init_done = [("S", b.val("S")), ("V", b.val("V"))]

    for w in ("wkvr", "wo", "cwr", "cwk", "cwv"):
        b.emit("gpsimd",
               (lambda ww: (lambda g: g.dma_start(
                   out=wscr[ww].rearrange("a b -> (a b)"),
                   in_=wdram[ww].rearrange("a b -> (a b)"))))(w),
               inc="PC_" + w, incn=16)

    def tern_pass_a(w, acc_name):
        mb = NFILLS[w]
        L = FCOLS[w]
        for m in range(mb):
            ld = b.emit("sync",
                        (lambda ww, mm, LL: (lambda s: s.dma_start(
                            out=sb["sx1"][:, 0:LL],
                            in_=wdram[ww][mm * 128:(mm + 1) * 128, :])))(w, m, L),
                        waits=[("V", b.val("V"))], inc="PB0", incn=16)
            b.emit("vector",
                   (lambda mm, LL, aa: (lambda v: v.tensor_reduce(
                       out=sb[aa][:, mm:mm + 1], in_=sb["sx1"][:, 0:LL],
                       axis=AX, op=ALU.add, apply_absolute_value=True)))(m, L, acc_name),
                   waits=[("PB0", ld)], inc="V")

    tern_pass_a("w1t", "accw1")
    tern_pass_a("w2t", "accw2")

    def tern_scale(acc_name, ncols, nelem, thr_name, col):
        b.emit("tensor",
               (lambda an, ncl: (lambda t: t.matmul(
                   pb[0][0:1, 0:ncl], sb["onesf"][:], sb[an][:, 0:ncl],
                   start=True, stop=True)))(acc_name, ncols),
               waits=wait_all([("V", b.val("V"))], bank_rel[0]), inc="T")
        b.emit("vector",
               (lambda ncl, cc: (lambda v: v.tensor_reduce(
                   out=sb["r_tern"][:, cc:cc + 1], in_=pb[0][0:1, 0:ncl],
                   axis=AX, op=ALU.add)))(ncols, col),
               waits=[("T", b.val("T"))], inc="V")
        bank_rel[0] = [("V", b.val("V"))]
        b.emit("vector",
               (lambda cc, ne: (lambda v: v.tensor_scalar_mul(
                   out=sb["r_tern"][:, cc + 1:cc + 2], in0=sb["r_tern"][:, cc:cc + 1],
                   scalar1=1.0 / ne)))(col, nelem),
               waits=[("V", b.val("V"))], inc="V")
        b.emit("vector",
               (lambda cc: (lambda v: v.tensor_scalar_max(
                   out=sb["r_tern"][:, cc + 1:cc + 2], in0=sb["r_tern"][:, cc + 1:cc + 2],
                   scalar1=1e-8)))(col),
               waits=[("V", b.val("V"))], inc="V")
        b.emit("vector",
               (lambda cc: (lambda v: v.tensor_scalar_mul(
                   out=sb["r_tern"][:, cc + 2:cc + 3], in0=sb["r_tern"][:, cc + 1:cc + 2],
                   scalar1=0.5)))(col),
               waits=[("V", b.val("V"))], inc="V")
        b.emit("vector",
               (lambda cc: (lambda v: v.tensor_scalar_mul(
                   out=sb["r_tern"][:, cc + 3:cc + 4], in0=sb["r_tern"][:, cc + 2:cc + 3],
                   scalar1=-1.0)))(col),
               waits=[("V", b.val("V"))], inc="V")
        b.emit("tensor",
               (lambda cc: (lambda t: t.matmul(
                   pb[0][:, 0:2], sb["onesrow"][:], sb["r_tern"][:, cc + 2:cc + 4],
                   start=True, stop=True)))(col),
               waits=wait_all([("V", b.val("V"))], bank_rel[0]), inc="T")
        b.emit("vector",
               (lambda tn: (lambda v: v.tensor_copy(out=sb[tn][:], in_=pb[0][:, 0:2])))(thr_name),
               waits=[("T", b.val("T"))], inc="V")
        bank_rel[0] = [("V", b.val("V"))]

    tern_scale("accw1", 16, 16 * 128 * 2048, "thr1", 0)
    tern_scale("accw2", 8, 8 * 128 * 4096, "thr2", 4)

    def tern_pass_b(w, thr_name):
        mb = NFILLS[w]
        L = FCOLS[w]
        nsub = L // NT
        prev_store = 0
        for m in range(mb):
            ld = b.emit("sync",
                        (lambda ww, mm, LL: (lambda s: s.dma_start(
                            out=sb["sx1"][:, 0:LL],
                            in_=wdram[ww][mm * 128:(mm + 1) * 128, :])))(w, m, L),
                        waits=[("V", b.val("V"))], inc="PB0", incn=16)
            for j in range(nsub):
                sl = slice(j * NT, (j + 1) * NT)
                b.emit("vector",
                       (lambda ss, tn: (lambda v: v.tensor_scalar(
                           out=sb["tr_m"][:], in0=sb["sx1"][:, ss], scalar1=sb[tn][:, 1:2],
                           scalar2=None, op0=ALU.is_le)))(sl, thr_name),
                       waits=wait_all([("PB0", ld)] if j == 0 else [],
                                      [("O", prev_store)] if j == 0 else []))
                b.emit("vector",
                       (lambda ss, tn: (lambda v: v.scalar_tensor_tensor(
                           out=sb["att"][:, ss], in0=sb["sx1"][:, ss], scalar=sb[tn][:, 0:1],
                           in1=sb["tr_m"][:], op0=ALU.is_ge, op1=ALU.subtract)))(sl, thr_name),
                       inc="V")
            prev_store = b.emit(
                "scalar",
                (lambda ww, mm, LL: (lambda s: s.dma_start(
                    out=wscr[ww][mm * 128:(mm + 1) * 128, :], in_=sb["att"][:, 0:LL])))(w, m, L),
                waits=[("V", b.val("V"))], inc="O", incn=16)

    tern_pass_b("w1t", "thr1")
    w1t_done = [("O", b.val("O"))]
    tern_pass_b("w2t", "thr2")
    w2t_done = [("O", b.val("O"))]
    att_free = [("O", b.val("O"))]
    prephase_v = [("V", b.val("V"))]

    FILLROWS = {"wk": 1, "wv": 1, "wr": 1, "wo": 1, "cwr": 1,
                "cwk": 2, "w1t": 2, "cwv": 1, "w2t": 1}

    def wfill(w, fill_idx):
        pool = WPOOL[w]
        ns, cols = POOLS[pool]
        assert cols == FCOLS[w]
        slot = slot_rr[pool] % ns
        slot_rr[pool] += 1
        key = f"{pool}{slot}"
        semn = f"W_{key}"
        src = wscr[w][fill_idx * 128:(fill_idx + 1) * 128, :]
        if w == "w1t":
            pre = w1t_done
        elif w == "w2t":
            pre = w2t_done
        else:
            pre = [("PC_" + w, 16)]
        q = WQUEUE[w]
        cnt = b.emit(q,
                     (lambda ss, kk: (lambda s: s.dma_start(
                         out=sb["ws_" + kk][:], in_=ss)))(src, key),
                     waits=wait_all(slot_rel[key], pre), inc=semn, incn=16)
        return [(semn, cnt)], key

    def wrel(key):
        slot_rel[key] = [("T", b.val("T"))]

    pend = {}

    def wfill_pre(w, idx):
        pend[(w, idx)] = wfill(w, idx)

    def wfill_get(w, idx):
        if (w, idx) in pend:
            return pend.pop((w, idx))
        return wfill(w, idx)

    def mm_acc(bank, lhsT_aps, rhs_aps, extra_waits=(), extra_at=()):
        n = len(lhsT_aps)
        ea = dict(extra_at)
        for i, (lt, rh) in enumerate(zip(lhsT_aps, rhs_aps)):
            w = []
            if i == 0:
                w = wait_all(list(extra_waits), bank_rel[bank])
            if i in ea:
                w = wait_all(w, ea[i])
            b.emit("tensor",
                   (lambda lt_, rh_, st_, sp_: (lambda t: t.matmul(
                       pb[bank][:], lt_, rh_, start=st_, stop=sp_)))(lt, rh, i == 0, i == n - 1),
                   waits=w, inc="T" if i == n - 1 else None)
        return [("T", b.val("T"))]

    def bank_done(bank, sem):
        bank_rel[bank] = [(sem, b.val(sem))]

    statT_chunk = {}

    def ln_stat_mms(ap_fn, first_waits, chunk_waits=None):
        for c in range(EC):
            par = c % 2
            sw = list(first_waits) if c == 0 else []
            if chunk_waits is not None:
                sw = wait_all(sw, chunk_waits(c))
            if c >= 2:
                sw = wait_all(sw, [("T", statT_chunk[c - 2])])
            b.emit("scalar",
                   (lambda aa, pp: (lambda s: s.activation(
                       out=vec_ap("sxbf", pp), in_=aa, func=AF.Copy)))(ap_fn(c), par),
                   waits=sw)
            b.emit("scalar",
                   (lambda aa, pp: (lambda s: s.activation(
                       out=vec_ap("ssqbf", pp), in_=aa, func=AF.Square)))(ap_fn(c), par),
                   inc="S")
            sv = b.val("S")
            w0 = [("S", sv)]
            if c == 0:
                w0 = wait_all(w0, bank_rel[0], bank_rel[1])
            b.emit("tensor",
                   (lambda pp, st_: (lambda t: t.matmul(
                       pb[0][0:1, :], sb["ones_bf"][:], vec_ap("sxbf", pp),
                       start=st_, stop=False)))(par, c == 0),
                   waits=w0)
            b.emit("tensor",
                   (lambda pp, st_, sp_: (lambda t: t.matmul(
                       pb[1][0:1, :], sb["ones_bf"][:], vec_ap("ssqbf", pp),
                       start=st_, stop=sp_)))(par, c == 0, c == EC - 1),
                   inc="T")
            statT_chunk[c] = b.val("T")
        return [("T", b.val("T"))]

    def ln_rows(fused_t1):
        b.emit("vector", lambda v: v.tensor_scalar(
            out=sb["r_mean"][:], in0=pb[4][0:1, :], scalar1=1.0 / E, scalar2=None,
            op0=ALU.mult), waits=[("T", b.val("T"))], inc="V")
        b.emit("vector", lambda v: v.tensor_scalar(
            out=sb["r_var"][:], in0=pb[7][0:1, :], scalar1=1.0 / E, scalar2=None,
            op0=ALU.mult), inc="V")
        b.emit("vector", lambda v: v.tensor_mul(sb["r_rstd"][:], sb["r_mean"][:], sb["r_mean"][:]),
               waits=[("V", b.val("V"))], inc="V")
        b.emit("vector", lambda v: v.tensor_sub(sb["r_var"][:], sb["r_var"][:], sb["r_rstd"][:]),
               waits=[("V", b.val("V"))], inc="V")
        bank_done(4, "V")
        bank_done(7, "V")
        vv = b.val("V")
        eps0 = LN0_EPS if fused_t1 else LN_EPS
        b.emit("scalar", (lambda e0: (lambda s: s.activation(
            out=sb["r_rstd"][:], in_=sb["r_var"][:], func=AF.Ln, bias=e0)))(eps0),
            waits=[("V", vv)], inc="S")
        b.emit("scalar", lambda s: s.activation(
            out=sb["r_rstd"][:], in_=sb["r_rstd"][:], func=AF.Exp, scale=-0.5),
            waits=[("S", b.val("S"))], inc="S")
        sv = b.val("S")
        if fused_t1:
            b.emit("vector", lambda v: v.tensor_mul(sb["r_var"][:], sb["r_var"][:], sb["r_rstd"][:]),
                   waits=[("S", sv)], inc="V")
            b.emit("vector", lambda v: v.tensor_mul(sb["r_var"][:], sb["r_var"][:], sb["r_rstd"][:]),
                   waits=[("V", b.val("V"))], inc="V")
            b.emit("vector", lambda v: v.tensor_mul(sb["r_mean"][:], sb["r_mean"][:], sb["r_rstd"][:]),
                   inc="V")
            vv = b.val("V")
            b.emit("scalar", lambda s: s.activation(
                out=sb["r_var"][:], in_=sb["r_var"][:], func=AF.Ln, bias=LN_EPS),
                waits=[("V", vv)], inc="S")
            b.emit("scalar", lambda s: s.activation(
                out=sb["r_var"][:], in_=sb["r_var"][:], func=AF.Exp, scale=-0.5),
                waits=[("S", b.val("S"))], inc="S")
        else:
            b.emit("vector", lambda v: v.tensor_mul(sb["r_mean"][:], sb["r_mean"][:], sb["r_rstd"][:]),
                   waits=[("S", sv)], inc="V")
        return [("S", b.val("S")), ("V", b.val("V"))]

    def ln_bcasts(rows_ready, banks_rows):
        for idx, (bk, row) in enumerate(banks_rows):
            b.emit("tensor",
                   (lambda bb, rr: (lambda t: t.matmul(
                       pb[bb][:], sb["onesrow"][:], sb[rr][:], start=True, stop=True)))(bk, row),
                   waits=wait_all(rows_ready if idx == 0 else [], bank_rel[bk]),
                   inc="T" if idx == len(banks_rows) - 1 else None)
        return [("T", b.val("T"))]


    ln2gv = {}

    def ln2_sq(c, x2v):
        b.emit("vector",
               (lambda cc: (lambda g: g.tensor_mul(
                   vec_ap("ssqbf", cc), vec_ap("sxn", cc), vec_ap("sxn", cc))))(c),
               waits=wait_all([("V", x2v[c])], fe.get("statT_done", []) if c == 0 else []),
               inc="G")
        ln2gv[c] = b.val("G")

    def ln2_mm_sum(c, x2v):
        w0 = [("V", x2v[c])]
        if c == 0:
            w0 = wait_all(w0, bank_rel[4])
        b.emit("tensor",
               (lambda cc, st_: (lambda t: t.matmul(
                   pb[4][0:1, :], sb["ones_bf"][:], vec_ap("sxn", cc),
                   start=st_, stop=False)))(c, c == 0),
               waits=w0, inc="T")

    def ln2_mm_sq(c):
        w0 = [("G", ln2gv[c])]
        if c == 0:
            w0 = wait_all(w0, bank_rel[7])
        b.emit("tensor",
               (lambda cc, st_, sp_: (lambda t: t.matmul(
                   pb[7][0:1, :], sb["ones_bf"][:], vec_ap("ssqbf", cc),
                   start=st_, stop=sp_)))(c, c == 0, c == EC - 1),
               waits=w0, inc="T")
        statT_chunk[c] = b.val("T")

    outdone = {}
    mix2_done_prev = []
    wkv_busy = []
    qv_hist = {}
    att_hist = {}
    div_hist = {}
    kcsq_busy = []
    fin_busy = []
    gel_busy = []
    hb = sb["hbuf"][:].rearrange("p (c n) -> p c n", c=EC)
    fe = {}

    def emit_xload(it):
        if it >= n_tiles:
            return
        xslot = "sx%d" % (it % 2)
        xsem = "X%d" % (it % 2)
        relw = []
        if it >= 2:
            relw = [("O", outdone[it - 2])]
        if it == 1:
            relw = wait_all(relw, prephase_v)
        cnt = b.emit("gpsimd",
                     (lambda xs, ii: (lambda s: s.dma_start(
                         out=sb[xs][:], in_=xS[ii])))(xslot, it),
                     waits=relw, inc=xsem, incn=16)
        fe["xrdy%d" % it] = [(xsem, cnt)]

    def fe_stat_gp(j, c, first_waits):
        xslot = "sx%d" % (j % 2)
        sw = wait_all(list(first_waits), fe["xrdy%d" % j] if c == 0 else [])
        b.emit("vector",
               (lambda xs, cc: (lambda g: g.tensor_mul(
                   vec_ap("ssqbf", cc), vec_ap(xs, cc), vec_ap(xs, cc))))(xslot, c),
               waits=sw, inc="G")
        fe["gv%d" % c] = b.val("G")

    def fe_stat_mm(j, c):
        xslot = "sx%d" % (j % 2)
        w0 = []
        if c == 0:
            w0 = wait_all(bank_rel[4], bank_rel[7], fe["xrdy%d" % j])
        b.emit("tensor",
               (lambda xs, cc, st_: (lambda t: t.matmul(
                   pb[4][0:1, :], sb["onesf"][:], vec_ap(xs, cc),
                   start=st_, stop=False)))(xslot, c, c == 0),
               waits=w0)
        b.emit("tensor",
               (lambda cc, st_, sp_: (lambda t: t.matmul(
                   pb[7][0:1, :], sb["ones_bf"][:], vec_ap("ssqbf", cc),
                   start=st_, stop=sp_)))(c, c == 0, c == EC - 1),
               waits=[("G", fe["gv%d" % c])], inc="T")
        statT_chunk[c] = b.val("T")

    def fe_rows():
        b.emit("vector", lambda v: v.tensor_scalar(
            out=sb["r_mean"][:], in0=pb[4][0:1, :], scalar1=1.0 / E, scalar2=None,
            op0=ALU.mult), waits=[("T", b.val("T"))], inc="V")
        b.emit("vector", lambda v: v.tensor_scalar(
            out=sb["r_var"][:], in0=pb[7][0:1, :], scalar1=1.0 / E, scalar2=None,
            op0=ALU.mult), inc="V")
        b.emit("vector", lambda v: v.tensor_mul(sb["r_rstd"][:], sb["r_mean"][:], sb["r_mean"][:]),
               waits=[("V", b.val("V"))], inc="V")
        b.emit("vector", lambda v: v.tensor_sub(sb["r_var"][:], sb["r_var"][:], sb["r_rstd"][:]),
               waits=[("V", b.val("V"))], inc="V")
        bank_done(4, "V")
        bank_done(7, "V")
        vv = b.val("V")
        b.emit("scalar", lambda s: s.activation(
            out=sb["r_rstd"][:], in_=sb["r_var"][:], func=AF.Ln, bias=LN0_EPS),
            waits=[("V", vv)], inc="S")
        b.emit("scalar", lambda s: s.activation(
            out=sb["r_rstd"][:], in_=sb["r_rstd"][:], func=AF.Exp, scale=-0.5),
            waits=[("S", b.val("S"))], inc="S")
        sv = b.val("S")
        b.emit("vector", lambda v: v.tensor_mul(sb["r_var"][:], sb["r_var"][:], sb["r_rstd"][:]),
               waits=[("S", sv)], inc="V")
        b.emit("vector", lambda v: v.tensor_mul(sb["r_var"][:], sb["r_var"][:], sb["r_rstd"][:]),
               waits=[("V", b.val("V"))], inc="V")
        b.emit("vector", lambda v: v.tensor_mul(sb["r_mean"][:], sb["r_mean"][:], sb["r_rstd"][:]),
               inc="V")
        vv = b.val("V")
        b.emit("scalar", lambda s: s.activation(
            out=sb["r_var"][:], in_=sb["r_var"][:], func=AF.Ln, bias=LN_EPS),
            waits=[("V", vv)], inc="S")
        b.emit("scalar", lambda s: s.activation(
            out=sb["r_var"][:], in_=sb["r_var"][:], func=AF.Exp, scale=-0.5),
            waits=[("S", b.val("S"))], inc="S")
        return [("S", b.val("S")), ("V", vv)]

    def fe_bcasts(rows_ready):
        return ln_bcasts(rows_ready, [(4, "r_rstd"), (7, "r_mean"), (0, "r_var")])

    def fe_boundary(j):
        b.emit("scalar",
               lambda s: s.copy(out=hb[:, :, 0:1],
                                in_=sb["hlast"][:].rearrange("p (c o) -> p c o", o=1)),
               waits=wait_all(mix2_done_prev, init_done), inc="S")
        fe["hlset"] = b.val("S")

    def fe_bchunk(j, c, bcast_rdy, xk_t_wait, d_t_wait):
        xslot = "sx%d" % (j % 2)
        b.emit("vector",
               (lambda xs, cc: (lambda v: v.tensor_mul(
                   sb["tr_m"][:], vec_ap(xs, cc), pb[4][:])))(xslot, c),
               waits=bcast_rdy if c == 0 else [])
        b.emit("vector",
               (lambda cc: (lambda v: v.tensor_sub(
                   vec_ap("sxn", cc), sb["tr_m"][:], pb[7][:])))(c))
        b.emit("vector",
               (lambda cc: (lambda v: v.tensor_mul(
                   sb["tr_m"][:], vec_ap("sxn", cc), pb[0][:])))(c))
        b.emit("vector",
               (lambda cc: (lambda v: v.tensor_scalar(
                   out=hb[:, cc, 1:NT + 1], in0=sb["tr_m"][:],
                   scalar1=sb["c_g1"][:, cc:cc + 1], scalar2=sb["c_b1v"][:, cc:cc + 1],
                   op0=ALU.mult, op1=ALU.add)))(c),
               waits=[("S", fe["hlset"])] if c == 0 else [], inc="V")
        dw = wait_all(xk_t_wait if c == 0 else [], d_t_wait if c == 0 else [])
        b.emit("vector",
               (lambda cc: (lambda v: v.tensor_sub(
                   sb["tr_k"][:], hb[:, cc, 1:NT + 1], hb[:, cc, 0:NT])))(c),
               waits=dw)
        for mname, dst in (("mk", "xk"), ("mv", "xv"), ("mr", "xr")):
            b.emit("vector",
                   (lambda cc, mn, dd: (lambda v: v.scalar_tensor_tensor(
                       out=vec_ap(dd, cc), in0=sb["tr_k"][:],
                       scalar=sb["c_" + mn][:, cc:cc + 1], in1=hb[:, cc, 0:NT],
                       op0=ALU.mult, op1=ALU.add)))(c, mname, dst),
                   inc="V" if dst == "xr" else None)

    def fe_finish(j):
        bank_done(4, "V")
        bank_done(7, "V")
        bank_done(0, "V")
        fe["mixes%d" % j] = [("V", b.val("V"))]
        fe["xnbf%d" % j] = [("V", b.val("V"))]
        b.emit("scalar",
               lambda s: s.copy(out=sb["hlast"][:].rearrange("p (c o) -> p c o", o=1),
                                in_=hb[:, :, NT:NT + 1]),
               waits=fe["mixes%d" % j], inc="S")

    def emit_div(gg):
        pqk = "tr_k%d" % (gg % 2)
        pqm = "tr_r%d" % (gg % 2)
        b.emit("scalar",
               (lambda pk_: (lambda s: s.activation(out=sb[pk_][:], in_=sb[pk_][:], func=AF.Ln)))(pqk),
               waits=[("V", qv_hist[gg])])
        b.emit("scalar",
               (lambda pm, pk_: (lambda s: s.activation(
                   out=sb[pm][:], in_=sb[pk_][:], func=AF.Exp, scale=-1.0)))(pqm, pqk),
               inc="S")
        div_hist[gg] = b.val("S")

    def emit_att(gg):
        pm = "tr_r%d" % (gg % 2)
        pe = "tr_e%d" % (gg % 2)
        aw = [("S", div_hist[gg])]
        if aw_att_free.get(gg):
            aw = wait_all(aw, aw_att_free[gg])
        b.emit("vector",
               (lambda g_, pm_, pe_: (lambda v: v.tensor_mul(
                   vec_ap("att", g_), sb[pe_][:], sb[pm_][:])))(gg, pm, pe),
               waits=aw, inc="V")
        att_hist[gg] = b.val("V")

    aw_att_free = {}

    emit_xload(0)
    emit_xload(1)
    for c in range(EC):
        fe_stat_gp(0, c, init_done if c == 0 else [])
    for c in range(EC):
        fe_stat_mm(0, c)
    fe["statT_done"] = [("T", b.val("T"))]
    rows_rdy = fe_rows()
    bc_rdy = fe_bcasts(rows_rdy)
    fe_boundary(0)
    for c in range(EC):
        fe_bchunk(0, c, bc_rdy, att_free, [])
    fe_finish(0)

    for it in range(n_tiles):
        xslot = "sx%d" % (it % 2)
        mixes_done = fe["mixes%d" % it]
        xnbf_all = fe["xnbf%d" % it]

        aw_att_free[0] = att_free if it == 0 else None
        for g in range(EC):
            if g == 2:
                wfill_pre("w1t", 0)
                wfill_pre("w1t", 1)
            if g == 4:
                wfill_pre("w1t", 2)
            kb, vb_, rb = (5, 6, 7) if g % 2 == 0 else (2, 3, 4)
            kp = "tr_k%d" % (g % 2)
            rp = "tr_r%d" % (g % 2)
            ep = "tr_e%d" % (g % 2)
            wk_rdy, wk_key = wfill_get("wkvr", g)
            lk = [sb["ws_" + wk_key][:, kc * 128:(kc + 1) * 128] for kc in range(EC)]
            lv = [sb["ws_" + wk_key][:, 1024 + kc * 128:1024 + (kc + 1) * 128] for kc in range(EC)]
            lr = [sb["ws_" + wk_key][:, 2048 + kc * 128:2048 + (kc + 1) * 128] for kc in range(EC)]
            rk = [vec_ap("xk", kc) for kc in range(EC)]
            rv = [vec_ap("xv", kc) for kc in range(EC)]
            rr = [vec_ap("xr", kc) for kc in range(EC)]
            kT = mm_acc(kb, lk, rk, extra_waits=wait_all(wk_rdy, mixes_done if g == 0 else []))
            b.emit("scalar",
                   (lambda bb, pp: (lambda s: s.activation(out=sb[pp][:], in_=pb[bb][:], func=AF.Exp)))(kb, kp),
                   waits=wait_all(kT, fin_busy if g == 0 else []))
            b.emit("scalar",
                   (lambda gg: (lambda s: s.copy(out=sb["snum"][:, 0:1], in_=sb["cnum"][:, gg:gg + 1])))(g),
                   waits=[("V", qv_hist[(g - 1) % EC])] if qv_hist.get((g - 1) % EC) else [])
            b.emit("scalar",
                   (lambda gg: (lambda s: s.copy(out=sb["sden"][:, 0:1], in_=sb["cden"][:, gg:gg + 1])))(g),
                   inc="S")
            ev_done = b.val("S")
            bank_done(kb, "S")
            vT = mm_acc(vb_, lv, rv)
            rT = mm_acc(rb, lr, rr)
            wrel(wk_key)
            b.emit("scalar",
                   (lambda bb, pp: (lambda s: s.activation(
                       out=sb[pp][:], in_=pb[bb][:], func=AF.Exp, scale=-1.0)))(rb, rp),
                   waits=wait_all(rT, [("V", att_hist[(g - 2) % EC])] if att_hist.get((g - 2) % EC) else []),
                   inc="S")
            bank_done(rb, "S")
            if g >= 1:
                emit_div(g - 1)

            ew_b, _ = bass.broadcast_tensor_aps(sb["ewc"][:, g:g + 1], sb[kp][:])
            b.emit("vector",
                   (lambda bb, pp: (lambda v: v.tensor_mul(sb["tr_m"][:], sb[pp][:], pb[bb][:])))(vb_, kp),
                   waits=wait_all([("S", ev_done)], vT), inc="V")
            bank_done(vb_, "V")
            b.emit("vector",
                   (lambda gg, pe: (lambda v: v.tensor_scalar(
                       out=sb[pe][:], in0=sb["tr_m"][:], scalar1=sb["euc"][:, gg:gg + 1],
                       scalar2=None, op0=ALU.mult)))(g, ep))
            if g >= 1:
                emit_att(g - 1)
            b.emit("vector",
                   (lambda gg, ewb: (lambda v: v.tensor_tensor_scan(
                       out=sb["snum"][:, 1:NT + 1], data0=ewb, data1=sb["tr_m"][:],
                       initial=sb["cnum"][:, gg:gg + 1], op0=ALU.mult, op1=ALU.add)))(g, ew_b))
            b.emit("vector",
                   (lambda gg, ewb, pp: (lambda v: v.tensor_tensor_scan(
                       out=sb["sden"][:, 1:NT + 1], data0=ewb, data1=sb[pp][:],
                       initial=sb["cden"][:, gg:gg + 1], op0=ALU.mult, op1=ALU.add)))(g, ew_b, kp))
            b.emit("vector",
                   (lambda pe: (lambda v: v.tensor_add(sb[pe][:], sb[pe][:], sb["snum"][:, 0:NT])))(ep),
                   waits=[("S", ev_done)])
            b.emit("vector",
                   (lambda gg, pp: (lambda v: v.scalar_tensor_tensor(
                       out=sb[pp][:], in0=sb[pp][:], scalar=sb["euc"][:, gg:gg + 1],
                       in1=sb["sden"][:, 0:NT], op0=ALU.mult, op1=ALU.add)))(g, kp))
            b.emit("vector",
                   (lambda pp, pr: (lambda v: v.scalar_tensor_tensor(
                       out=sb[pp][:], in0=sb[pr][:], scalar=1.0, in1=sb[pp][:],
                       op0=ALU.add, op1=ALU.mult)))(kp, rp), inc="V")
            qv_hist[g] = b.val("V")
            b.emit("scalar",
                   (lambda gg: (lambda s: s.copy(out=sb["cnum"][:, gg:gg + 1], in_=sb["snum"][:, NT:NT + 1])))(g),
                   waits=[("V", qv_hist[g])])
            b.emit("scalar",
                   (lambda gg: (lambda s: s.copy(out=sb["cden"][:, gg:gg + 1], in_=sb["sden"][:, NT:NT + 1])))(g),
                   inc="S")
        for f2 in range(FC // 2):
            w1_rdy, w1_key = wfill_get("w1t", f2)
            if f2 == 4:
                wfill_pre("w2t", 0)
                wfill_pre("w2t", 1)
            for half in range(2):
                f = 2 * f2 + half
                hbk = (0, 1, 5, 6)[f % 4]
                lw = [sb["ws_" + w1_key][:, half * 1024 + kc * 128: half * 1024 + (kc + 1) * 128]
                      for kc in range(EC)]
                rx = [vec_ap("sxn", kc) for kc in range(EC)]
                hT = mm_acc(hbk, lw, rx,
                            extra_waits=wait_all(w1_rdy if half == 0 else [],
                                                 xnbf_all if f == 0 else []))
                b.emit("scalar",
                       (lambda ff, bb: (lambda s: s.activation(
                           out=vec_ap("bigbf", ff), in_=pb[bb][:], func=AF.Gelu,
                           bias=sb["c_b1f"][:, ff:ff + 1])))(f, hbk),
                       waits=wait_all(hT, gel_busy), inc="S")
                bank_done(hbk, "S")
                gel_busy = [("S", b.val("S"))]
            if f2 == 0:
                emit_div(EC - 1)
                emit_att(EC - 1)
                att_done = [("V", b.val("V"))]
            wrel(w1_key)
        gel_done = [("S", b.val("S"))]

        nxt = it + 1
        have_next = nxt < n_tiles
        x2v = {}
        for c2 in range(EC // 2):
            wo_rdy, wo_key = wfill("wo", c2)
            for half in range(2):
                c = 2 * c2 + half
                yb = 5 if c % 2 == 0 else 6
                lo = [sb["ws_" + wo_key][:, half * 1024 + kc * 128: half * 1024 + (kc + 1) * 128]
                      for kc in range(EC)]
                ro = [vec_ap("att", kc) for kc in range(EC)]
                yT = mm_acc(yb, lo, ro,
                            extra_waits=wait_all(wo_rdy if half == 0 else [],
                                                 att_done if c == 0 else []))
                b.emit("vector",
                       (lambda cc, bb: (lambda v: v.tensor_add(
                           vec_ap("sxn", cc), vec_ap("sxn", cc), pb[bb][:])))(c, yb),
                       waits=yT, inc="V")
                bank_done(yb, "V")
                x2v[c] = b.val("V")
                ln2_sq(c, x2v)
                if c >= 1:
                    ln2_mm_sum(c - 1, x2v)
                if c >= 3:
                    ln2_mm_sq(c - 3)
            wrel(wo_key)

        wfill_pre("cwr", 0)
        wfill_pre("cwr", 1)
        b.emit("scalar",
               lambda s: s.copy(out=hb[:, :, 0:1],
                                in_=sb["h2last"][:].rearrange("p (c o) -> p c o", o=1)),
               waits=mixes_done, inc="S")
        h2l_set = b.val("S")
        bcast2 = None
        rows2 = None

        def h2_chunk(cc):
            b.emit("vector",
                   (lambda c_: (lambda v: v.tensor_sub(sb["tr_m"][:], vec_ap("sxn", c_), pb[3][:])))(cc),
                   waits=bcast2 if cc == 0 else [])
            b.emit("vector",
                   lambda v: v.tensor_mul(sb["tr_m"][:], sb["tr_m"][:], pb[2][:]))
            b.emit("vector",
                   (lambda c_: (lambda v: v.tensor_scalar(
                       out=hb[:, c_, 1:NT + 1], in0=sb["tr_m"][:],
                       scalar1=sb["c_g2"][:, c_:c_ + 1], scalar2=sb["c_b2v"][:, c_:c_ + 1],
                       op0=ALU.mult, op1=ALU.add)))(cc),
                   waits=[("S", h2l_set)] if cc == 0 else [], inc="V")
            b.emit("vector",
                   (lambda c_: (lambda v: v.tensor_sub(
                       sb["tr_k"][:], hb[:, c_, 1:NT + 1], hb[:, c_, 0:NT])))(cc))
            for mname, dst in (("cmk", "xk"), ("cmr", "xr")):
                b.emit("vector",
                       (lambda c_, mn, dd: (lambda v: v.scalar_tensor_tensor(
                           out=vec_ap(dd, c_), in0=sb["tr_k"][:],
                           scalar=sb["c_" + mn][:, c_:c_ + 1], in1=hb[:, c_, 0:NT],
                           op0=ALU.mult, op1=ALU.add)))(cc, mname, dst),
                       inc="V" if dst == "xr" else None)

        for c in range(EC):
            ob = 0 if c % 2 == 0 else 1
            h0_rdy, h0_key = wfill_get("w2t", c)
            lhs = [sb["ws_" + h0_key][:, kk * 128:(kk + 1) * 128] for kk in range(32)]
            rhs = [vec_ap("bigbf", kk) for kk in range(32)]
            oT = mm_acc(ob, lhs, rhs,
                        extra_waits=wait_all(h0_rdy, gel_done if c == 0 else []))
            wrel(h0_key)
            if c == 0:
                ln2_mm_sum(7, x2v)
                ln2_mm_sq(5)
                ln2_mm_sq(6)
                ln2_mm_sq(7)
                rows2 = ln_rows(False)
                if have_next:
                    for cc in range(EC):
                        fe_stat_gp(nxt, cc, [("T", statT_chunk[cc])])
            elif c == 1:
                bcast2 = ln_bcasts(rows2, [(2, "r_rstd"), (3, "r_mean")])
            b.emit("vector",
                   (lambda cc, bb: (lambda v: v.scalar_tensor_tensor(
                       out=sb["tr_m"][:], in0=pb[bb][:], scalar=sb["c_b2f"][:, cc:cc + 1],
                       in1=vec_ap("sxn", cc), op0=ALU.add, op1=ALU.add)))(c, ob),
                   waits=oT)
            b.emit("vector",
                   (lambda cc, xs: (lambda v: v.tensor_add(
                       vec_ap(xs, cc), vec_ap(xs, cc), sb["tr_m"][:])))(c, xslot),
                   inc="V")
            bank_done(ob, "V")
            if c >= 1:
                h2_chunk(c - 1)
        h2_chunk(EC - 1)
        bank_done(2, "V")
        bank_done(3, "V")
        mix2_done = [("V", b.val("V"))]
        mix2_done_prev = mix2_done
        b.emit("scalar",
               lambda s: s.copy(out=sb["h2last"][:].rearrange("p (c o) -> p c o", o=1),
                                in_=hb[:, :, NT:NT + 1]),
               waits=mix2_done, inc="S")
        wfill_pre("cwk", 0)
        wfill_pre("cwk", 1)

        for f2 in range(FC // 2):
            cwk_rdy, cwk_key = wfill_get("cwk", f2)
            for half in range(2):
                f = 2 * f2 + half
                kbk = 0 if f % 2 == 0 else 1
                lw = [sb["ws_" + cwk_key][:, half * 1024 + kc * 128: half * 1024 + (kc + 1) * 128]
                      for kc in range(EC)]
                rk = [vec_ap("xk", kc) for kc in range(EC)]
                kTd = mm_acc(kbk, lw, rk,
                             extra_waits=wait_all(cwk_rdy if half == 0 else [],
                                                  mix2_done if f == 0 else []))
                b.emit("vector",
                       (lambda bb: (lambda v: v.tensor_scalar(
                           out=sb["tr_k"][:], in0=pb[bb][:], scalar1=0.0, scalar2=None,
                           op0=ALU.max)))(kbk),
                       waits=wait_all(kTd, kcsq_busy), inc="V")
                bank_done(kbk, "V")
                b.emit("scalar",
                       (lambda ff: (lambda s: s.activation(
                           out=vec_ap("bigbf", ff), in_=sb["tr_k"][:], func=AF.Square)))(f),
                       waits=[("V", b.val("V"))], inc="S")
                kcsq_busy = [("S", b.val("S"))]
            wrel(cwk_key)
            if have_next and 4 <= f2 <= 11:
                fe_stat_mm(nxt, f2 - 4)
        kc_done = [("S", b.val("S"))]
        if have_next:
            wfill_pre("wkvr", 0)
            wfill_pre("wkvr", 1)
            fe["statT_done"] = [("T", b.val("T"))]
            fe["rows_rdy"] = fe_rows()

        wfill_pre("cwv", 0)
        wfill_pre("cwv", 1)
        for c in range(EC):
            rbk = 2 if c % 2 == 0 else 3
            if c % 2 == 0:
                cwr_rdy, cwr_key = wfill_get("cwr", c // 2)
            half = c % 2
            lr = [sb["ws_" + cwr_key][:, half * 1024 + kc * 128: half * 1024 + (kc + 1) * 128]
                  for kc in range(EC)]
            rr = [vec_ap("xr", kc) for kc in range(EC)]
            rTd = mm_acc(rbk, lr, rr,
                         extra_waits=wait_all(cwr_rdy if half == 0 else [],
                                              att_done if c == 0 else []))
            if half == 1:
                wrel(cwr_key)
            b.emit("scalar",
                   (lambda bb: (lambda s: s.activation(
                       out=sb["tr_r"][:], in_=pb[bb][:], func=AF.Exp, scale=-1.0)))(rbk),
                   waits=wait_all(rTd, fin_busy if c == 0 else [],
                                  att_free if (it == 0 and c == 0) else []))
            b.emit("scalar", lambda s: s.activation(
                out=sb["tr_r"][:], in_=sb["tr_r"][:], func=AF.Ln, bias=1.0))
            b.emit("scalar",
                   (lambda cc: (lambda s: s.activation(
                       out=vec_ap("att", cc), in_=sb["tr_r"][:], func=AF.Exp, scale=-1.0)))(c),
                   inc="S")
            bank_done(rbk, "S")
        rq_done = [("S", b.val("S"))]
        cwr_T_done = [("T", b.val("T"))]
        if have_next:
            fe["bc_rdy"] = fe_bcasts(fe["rows_rdy"])

        if have_next:
            fe_boundary(nxt)
        for c in range(EC):
            ob = 5 if c % 2 == 0 else 6
            h0_rdy, h0_key = wfill_get("cwv", c)
            lhs = [sb["ws_" + h0_key][:, kk * 128:(kk + 1) * 128] for kk in range(32)]
            rhs = [vec_ap("bigbf", kk) for kk in range(32)]
            kcvT = mm_acc(ob, lhs, rhs,
                          extra_waits=wait_all(h0_rdy, kc_done if c == 0 else []))
            wrel(h0_key)
            if have_next:
                fe_bchunk(nxt, c, fe["bc_rdy"],
                          cwr_T_done,
                          [("T", statT_chunk[EC - 1])])
            b.emit("vector",
                   (lambda bb, cc: (lambda v: v.tensor_mul(
                       sb["tr_e0"][:], pb[bb][:], vec_ap("att", cc))))(ob, c),
                   waits=wait_all(kcvT, rq_done if c == 0 else []))
            b.emit("vector",
                   (lambda cc, xs: (lambda v: v.tensor_add(
                       vec_ap(xs, cc), vec_ap(xs, cc), sb["tr_e0"][:])))(c, xslot),
                   inc="V")
            bank_done(ob, "V")
            fin_busy = [("V", b.val("V"))]
        fin_all = [("V", b.val("V"))]
        if have_next:
            fe_finish(nxt)

        outdone[it] = b.emit("gpsimd",
                             (lambda xs, ii: (lambda s: s.dma_start(
                                 out=outS[ii], in_=sb[xs][:])))(xslot, it),
                             waits=fin_all, inc="O", incn=16)
        emit_xload(it + 2)

    b.emit("gpsimd", lambda g: g.engine_nop(), waits=[("O", b.val("O"))])

    with nc.Block() as block:
        def realize(eng_name):
            def body(eng):
                for ws, fn, inc, incn in b.ev[eng_name]:
                    for s, v in ws:
                        eng.wait_ge(sems[s], v)
                    instr = fn(eng) if fn is not None else None
                    if inc is not None:
                        if instr is not None:
                            instr.then_inc(sems[inc], incn)
                        else:
                            eng.sem_inc(sems[inc], incn)
            return body

        block.tensor(realize("tensor"))
        block.vector(realize("vector"))
        block.scalar(realize("scalar"))
        block.sync(realize("sync"))
        block.gpsimd(realize("gpsimd"))

    ctx.close()
    return nc


def _stream_layout(w, kc, mb, pair=False, halfk=False):
    K, M = w.shape
    assert K == kc * 128 and M == mb * 128
    a = w.reshape(kc, 128, mb, 128)
    if halfk:
        h = kc // 2
        a = a.reshape(2, h, 128, mb, 128)
        out = a.transpose(3, 0, 2, 1, 4)
        return np.ascontiguousarray(out.reshape(mb * 2 * 128, h * 128))
    if pair:
        a = a.reshape(kc, 128, mb // 2, 2, 128)
        out = a.transpose(2, 1, 3, 0, 4)
        return np.ascontiguousarray(out.reshape((mb // 2) * 128, 2 * kc * 128))
    out = a.transpose(2, 1, 0, 3)
    return np.ascontiguousarray(out.reshape(mb * 128, kc * 128))


def _vec_layout(v):
    n = v.shape[0] // 128
    return np.ascontiguousarray(v.reshape(n, 128).T)


def _x_stream(x2d, t_total=T_TOTAL):
    n_tiles = t_total // NT
    a = x2d.T.reshape(EC, 128, n_tiles, NT)
    return np.ascontiguousarray(a.transpose(2, 1, 0, 3).reshape(n_tiles, 128, EC * NT))


def _x_unstream(s, t_total=T_TOTAL):
    n_tiles = t_total // NT
    a = s.reshape(n_tiles, 128, EC, NT).transpose(2, 1, 0, 3)
    return np.ascontiguousarray(a.reshape(E, t_total).T)


_prog_cache = {}


def _prep_weights(inp):
    f32 = lambda a: np.asarray(a, dtype=np.float32)
    wk = _stream_layout(f32(inp["L0_attn_tm_Wk"]), 8, 8)
    wv = _stream_layout(f32(inp["L0_attn_tm_Wv"]), 8, 8)
    wr = _stream_layout(f32(inp["L0_attn_tm_Wr"]), 8, 8)
    return {
        "wkvr": np.ascontiguousarray(np.concatenate([wk, wv, wr], axis=1)),
        "wo": _stream_layout(f32(inp["L0_attn_Wo"]), 8, 8, pair=True),
        "cwr": _stream_layout(f32(inp["L0_attn_cm_Wr"]), 8, 8, pair=True),
        "cwk": _stream_layout(f32(inp["L0_attn_cm_Wk"]), 8, 32, pair=True),
        "w1t": _stream_layout(np.ascontiguousarray(f32(inp["L0_ffn_W1"]).T), 8, 32, pair=True),
        "cwv": _stream_layout(f32(inp["L0_attn_cm_Wv"]), 32, 8),
        "w2t": _stream_layout(np.ascontiguousarray(f32(inp["L0_ffn_W2"]).T), 32, 8),
    }


def _prep_vecs(inp):
    f32 = lambda a: np.asarray(a, dtype=np.float32)
    return {
        "g1": _vec_layout(f32(inp["L0_attn_ln1_g"])), "b1v": _vec_layout(f32(inp["L0_attn_ln1_b"])),
        "g2": _vec_layout(f32(inp["L0_attn_ln2_g"])), "b2v": _vec_layout(f32(inp["L0_attn_ln2_b"])),
        "mk": _vec_layout(f32(inp["L0_attn_tm_mix_k"])), "mv": _vec_layout(f32(inp["L0_attn_tm_mix_v"])),
        "mr": _vec_layout(f32(inp["L0_attn_tm_mix_r"])), "cmk": _vec_layout(f32(inp["L0_attn_cm_mix_k"])),
        "cmr": _vec_layout(f32(inp["L0_attn_cm_mix_r"])), "td": _vec_layout(f32(inp["L0_attn_time_decay"])),
        "tf": _vec_layout(f32(inp["L0_attn_time_first"])), "b2f": _vec_layout(f32(inp["L0_ffn_b2"])),
        "b1f": _vec_layout(f32(inp["L0_ffn_b1"])),
    }


def kernel(
    L0_attn_inputs, L0_attn_ln1_g, L0_attn_ln1_b, L0_attn_ln2_g, L0_attn_ln2_b,
    L0_attn_tm_mix_k, L0_attn_tm_mix_v, L0_attn_tm_mix_r, L0_attn_tm_Wk,
    L0_attn_tm_Wv, L0_attn_tm_Wr, L0_attn_time_decay, L0_attn_time_first,
    L0_attn_Wo, L0_attn_cm_mix_k, L0_attn_cm_mix_r, L0_attn_cm_Wk,
    L0_attn_cm_Wv, L0_attn_cm_Wr, L0_ffn_W1, L0_ffn_b1, L0_ffn_W2, L0_ffn_b2,
):
    x = np.asarray(L0_attn_inputs, dtype=np.float32)
    B, T, E_ = x.shape
    assert (T, E_) == (T_TOTAL, E) and B == N_CORES

    inp = dict(
        L0_attn_tm_Wk=L0_attn_tm_Wk, L0_attn_tm_Wv=L0_attn_tm_Wv,
        L0_attn_tm_Wr=L0_attn_tm_Wr, L0_attn_Wo=L0_attn_Wo,
        L0_attn_cm_Wr=L0_attn_cm_Wr, L0_attn_cm_Wk=L0_attn_cm_Wk,
        L0_attn_cm_Wv=L0_attn_cm_Wv, L0_ffn_W1=L0_ffn_W1, L0_ffn_W2=L0_ffn_W2,
        L0_attn_ln1_g=L0_attn_ln1_g, L0_attn_ln1_b=L0_attn_ln1_b,
        L0_attn_ln2_g=L0_attn_ln2_g, L0_attn_ln2_b=L0_attn_ln2_b,
        L0_attn_tm_mix_k=L0_attn_tm_mix_k, L0_attn_tm_mix_v=L0_attn_tm_mix_v,
        L0_attn_tm_mix_r=L0_attn_tm_mix_r, L0_attn_cm_mix_k=L0_attn_cm_mix_k,
        L0_attn_cm_mix_r=L0_attn_cm_mix_r, L0_attn_time_decay=L0_attn_time_decay,
        L0_attn_time_first=L0_attn_time_first, L0_ffn_b1=L0_ffn_b1,
        L0_ffn_b2=L0_ffn_b2,
    )
    wmats = _prep_weights(inp)
    vecs = _prep_vecs(inp)

    if "nc" not in _prog_cache:
        _prog_cache["nc"] = build_program()
    nc = _prog_cache["nc"]

    in_maps = []
    for bI in range(N_CORES):
        m = {"xS": _x_stream(x[bI])}
        m.update(wmats)
        m.update(vecs)
        in_maps.append(m)

    res = run_bass_kernel_spmd(nc, in_maps, core_ids=list(range(N_CORES)))
    out = np.stack([_x_unstream(np.asarray(r["outS"])) for r in res.results], axis=0)
    return out.astype(np.float32)
